# version 12
# baseline (speedup 1.0000x reference)
import contextlib

import numpy as np

import concourse.bass as bass
import concourse.mybir as mybir
from concourse.bass_utils import run_bass_kernel_spmd

F32 = mybir.dt.float32
BF16 = mybir.dt.bfloat16
I8 = mybir.dt.int8
U8 = mybir.dt.uint8
AF = mybir.ActivationFunctionType
ALU = mybir.AluOpType
AX = mybir.AxisListType

N_CORES = 8
S_TOT = 8192
T = 1024
D = 2048
DFF = 8192
HC = 16
NQB = 8
NKB = 16
NI = NQB * NKB
C_SCALE = float(1.0 / np.sqrt(np.float32(2048.0), dtype=np.float32))
LN_EPS = 1e-5
Q_EPS = 1e-5
INV127 = float(np.float32(1.0) / np.float32(127.0))
INV_D = float(np.float32(1.0) / np.float32(2048.0))
WN_INV = float(np.float32(1.0) / np.float32(8192 * 2048))
CLIP = 1.4999999
TH = 512
KB = 1024

DEBUG = []


class Ctr:
    def __init__(self, handle):
        self.h = handle
        self.n = 0

    def inc(self, inst, k=1):
        inst.then_inc(self.h, k)
        self.n += k
        return self.n

    def dma(self, inst):
        return self.inc(inst, 16)


def build(debug=()):
    nc = bass.Bass("TRN2", target_bir_lowering=False, debug=False)
    ctx = contextlib.ExitStack()

    xT = nc.dram_tensor("xT", [D, T], F32, kind="ExternalInput").ap()
    wqkvT = nc.dram_tensor("wqkvT", [D, 3 * D], F32, kind="ExternalInput").ap()
    b_q = nc.dram_tensor("b_q", [128, HC], F32, kind="ExternalInput").ap()
    b_k = nc.dram_tensor("b_k", [128, HC], F32, kind="ExternalInput").ap()
    b_v_bc = nc.dram_tensor("b_v_bc", [128, D], F32, kind="ExternalInput").ap()
    woutT = nc.dram_tensor("woutT", [D, D], F32, kind="ExternalInput").ap()
    b_o = nc.dram_tensor("b_o", [128, HC], F32, kind="ExternalInput").ap()
    wgT_sh = nc.dram_tensor("wgT_sh", [256, DFF], F32, kind="ExternalInput").ap()
    wuT_sh = nc.dram_tensor("wuT_sh", [256, DFF], F32, kind="ExternalInput").ap()
    wdT_sh = nc.dram_tensor("wdT_sh", [1024, D], F32, kind="ExternalInput").ap()
    identI = nc.dram_tensor("identI", [128, 128], F32, kind="ExternalInput").ap()
    out = nc.dram_tensor("out", [D, T], F32, kind="ExternalOutput").ap()

    kvKV_in = nc.dram_tensor("kvKV_in", [2 * D, T], BF16).ap()
    kvKV_out = nc.dram_tensor("kvKV_out", [N_CORES * 2 * D, T], BF16).ap()
    ar_in = nc.dram_tensor("ar_in", [1, 8], F32).ap()
    ar_out = nc.dram_tensor("ar_out", [1, 8], F32).ap()
    tern_in = nc.dram_tensor("tern_in", [1536, 4096], I8).ap()
    tern_out = nc.dram_tensor("tern_out", [N_CORES * 1536, 4096], I8).ap()
    x1_sc = nc.dram_tensor("x1_sc", [D, T], F32).ap()

    dbg = {}
    for name, shape, dt in (
        ("qt", [128, HC * T], BF16), ("kv", [N_CORES * 2 * D, T], BF16),
        ("s", [128, S_TOT], F32), ("ctxT", [128, HC * T], BF16),
        ("x1", [128, HC * T], F32), ("xq", [128, HC * T], BF16),
        ("tern", [N_CORES * 1536, 4096], I8), ("h", [128, 64 * TH], BF16),
        ("m0", [128, 16], F32), ("s0", [128, S_TOT], F32),
    ):
        if name in debug:
            dbg[name] = nc.dram_tensor("dbg_" + name, shape, dt,
                                       kind="ExternalOutput").ap()

    ARENA_B = 204 * KB
    arena = ctx.enter_context(nc.sbuf_tensor("arena", [128, ARENA_B], U8))
    arena_base = nc.lookup_mloc(arena).addr

    def at(name, shape, dtype, off):
        per_part = int(np.prod(shape[1:])) * mybir.dt.size(dtype)
        assert off % 64 == 0 and off + per_part <= ARENA_B, (name, off, per_part)
        h = nc.virtual_sbuf_tensor(name, shape, dtype)
        nc._tensor(h.name, h.shape, h.dtype, type="SB")
        mloc = nc.lookup_mloc(h)
        mloc.allocated = True
        mloc.addr = arena_base + off
        return h

    ones_f = at("ones_f", [128, 128], F32, 0)
    ones_b = at("ones_b", [128, 128], BF16, 512)
    ident = at("ident", [128, 128], F32, 768)
    bq_s = at("bq_s", [128, HC], F32, 1280)
    bk_s = at("bk_s", [128, HC], F32, 1344)
    bo_s = at("bo_s", [128, HC], F32, 1408)
    svec = at("svec", [128, 8], F32, 1472)
    ivec = at("ivec", [128, 8], F32, 1536)
    arst = at("arst", [128, 16], F32, 1600)
    parts = at("parts", [128, 32], F32, 1664)
    mneg = at("mneg", [128, 1], F32, 1856)
    linv = at("linv", [128, 1], F32, 1920)
    stok = at("stok", [128, 16], F32, 1984)
    itok = at("itok", [128, 16], F32, 2048)
    btile = at("btile", [128, 128], F32, 2112)
    mb_t = at("mb_t", [128, 16], F32, 2624)
    tmp1k = at("tmp1k", [128, 1024], F32, 2688)
    i8tmp = at("i8tmp", [128, 1024], I8, 6784)
    kstage = at("kstage", [128, 2048], BF16, 8192)
    p_bf = at("p_bf", [128, 1024], BF16, 8192)
    pT_b = at("pT_b", [128, 1024], BF16, 10240)
    rbf_t = at("rbf_t", [128, 2 * 1024], BF16, 8192)
    r2bf_t = at("r2bf_t", [128, 2 * 1024], BF16, 12288)
    sgt = at("sgt", [128, 512], F32, 8192)
    upt = at("upt", [128, 512], F32, 10240)
    vstage = at("vstage", [128, 1024], BF16, 12288)
    ctxbf = at("ctxbf", [128, 2048], BF16, 12288)
    x1re = at("x1re", [128, 512], F32, 12288)
    aln = at("aln", [128, 512], F32, 14336)
    wqb = at("wqb", [128, 2 * D], BF16, 16384)
    wvb = at("wvb", [128, 1024], BF16, 16384)
    wob = at("wob", [128, 2 * D], BF16, 16384)
    wdb = at("wdb", [128, 2048], BF16, 16384)
    cln = at("cln", [128, 512], F32, 20480)
    rb2 = at("rb2", [128, 512], BF16, 22528)
    r2b2 = at("r2b2", [128, 512], BF16, 23552)
    hacc = at("hacc", [128, 512], F32, 24576)
    hacc2 = at("hacc2", [128, 512], F32, 28672)
    obuf = at("obuf", [128, 512], F32, 26624)
    B0 = 30 * KB
    wqf32 = at("wqf32", [128, 2 * 2048], F32, B0 + 32 * KB)
    vf32 = at("vf32", [128, 8 * 512], F32, B0 + 48 * KB)
    xt_s = at("xt_s", [128, HC * 1024], F32, B0)
    kbuf = at("kbuf", [128, 2 * HC * 512], BF16, B0)
    vbuf = at("vbuf", [128, 2 * 4 * D], BF16, B0 + 32 * KB)
    h_s = at("h_s", [128, 64 * TH], BF16, B0)
    C0 = 94 * KB
    ternb = at("ternb", [128, 2 * 2048], F32, C0)
    terni = at("terni", [128, 2 * 2048], I8, C0 + 16 * KB)
    bvbc = at("bvbc", [128, D], F32, C0 + 20 * KB)
    S_s = at("S_s", [128, S_TOT], F32, C0)
    xq_s = at("xq_s", [128, HC * 1024], BF16, C0)
    xbf_s = at("xbf_s", [128, HC * 1024], BF16, C0 + 32 * KB)
    ctxT_s = at("ctxT_s", [128, HC * 1024], BF16, C0 + 32 * KB)
    wgub = at("wgub", [128, 2 * 2 * 4096], BF16, C0 + 32 * KB)
    D0 = 158 * KB
    qT_s = at("qT_s", [128, HC * 1024], BF16, D0)
    sxbc = at("sxbc", [128, 1024], F32, D0)
    ixbc = at("ixbc", [128, 1024], F32, D0 + 4 * KB)
    a1_t = at("a1_t", [128, 1024], F32, D0 + 8 * KB)
    c1_t = at("c1_t", [128, 1024], F32, D0 + 12 * KB)
    acc_t = at("acc_t", [128, 1024], F32, D0 + 16 * KB)
    acc2_t = at("acc2_t", [128, 1024], F32, D0 + 20 * KB)
    wof32 = at("wof32", [128, 2048], F32, D0 + 24 * KB)
    r2_s = at("r2_s", [128, HC * TH], F32, D0)
    bg_t = at("bg_t", [128, 1024], F32, D0 + 32 * KB)
    bu_t = at("bu_t", [128, 1024], F32, D0 + 36 * KB)
    shb = at("shb", [128, 512], F32, D0 + 40 * KB)
    ihb = at("ihb", [128, 512], F32, D0 + 42 * KB)
    dsc = at("dsc", [128, 512], F32, D0 + 44 * KB)

    PP = ctx.enter_context(nc.psum_tensor("PP", [128, 4096], F32))

    def bank(i, w=512):
        return PP[:, 512 * i:512 * i + w]

    sem = {}
    for s in ("dS", "dQ", "dG", "dW", "pe", "p2", "dv", "ac", "cc"):
        sem[s] = Ctr(ctx.enter_context(nc.semaphore("sem_" + s)))
    dS, dQ, dG, dW = sem["dS"], sem["dQ"], sem["dG"], sem["dW"]
    pe, p2, dv, ac, cc = sem["pe"], sem["p2"], sem["dv"], sem["ac"], sem["cc"]

    blk = ctx.enter_context(nc.Block())
    M = {}

    @blk.gpsimd
    def _(g):
        dG.inc(g.memset(ones_f[:], 1.0))
        dG.inc(g.memset(ones_b[:], 1.0))
        dG.inc(g.memset(arst[:], 0.0))
        dG.inc(g.memset(parts[:], 0.0))

    @blk.sync
    def _(s):
        dS.dma(s.dma_start(out=xt_s[:].rearrange("p (c t) -> p c t", t=1024),
                           in_=xT.rearrange("(c p) t -> p c t", p=128)))
        dS.dma(s.dma_start(out=ident[:], in_=identI[:, :]))
        dS.dma(s.dma_start(out=bq_s[:], in_=b_q[:, :]))
        dS.dma(s.dma_start(out=bk_s[:], in_=b_k[:, :]))
        dS.dma(s.dma_start(out=bo_s[:], in_=b_o[:, :]))
        dS.dma(s.dma_start(out=bvbc[:], in_=b_v_bc[:, :]))
    M["x_ld"] = dS.n

    def tern_ld_aps(i):
        m, sl = divmod(i, 8)
        buf = ternb[:, 2048 * (i % 2):2048 * (i % 2) + 2048]
        if m < 2:
            wsh = wgT_sh if m == 0 else wuT_sh
            src = wsh[:, 1024 * sl:1024 * (sl + 1)].rearrange(
                "(n p) w -> p n w", p=128)
            dst = buf.rearrange("p (n w) -> p n w", w=1024)
        else:
            g2, c = divmod(sl, 2)
            src = wdT_sh[256 * g2:256 * (g2 + 1),
                         1024 * c:1024 * (c + 1)].rearrange(
                "(p par) o -> p par o", par=2)
            dst = buf.rearrange("p (par o) -> p par o", o=1024)
        return src, dst

    def tern_st_aps(i):
        m, sl = divmod(i, 8)
        buf = terni[:, 2048 * (i % 2):2048 * (i % 2) + 2048]
        if m < 2:
            base = 512 * m
            dst = tern_in[base + sl // 4:base + 512:2,
                          1024 * (sl % 4):1024 * (sl % 4) + 1024].rearrange(
                "(n p) w -> p n w", p=128)
            src = buf.rearrange("p (n w) -> p n w", w=1024)
        else:
            g2, c = divmod(sl, 2)
            dst = tern_in[1024 + 128 * g2:1024 + 128 * (g2 + 1), :].rearrange(
                "r (par z) -> r par z", z=2048)[:, :, 1024 * c:1024 * (c + 1)]
            src = buf.rearrange("p (par o) -> p par o", o=1024)
        return dst, src

    TRED_AT = {8 + 2 * _k: _k for _k in range(24)}
    _dv = 36
    for _i in range(64):
        if _i + 1 < 64:
            _dv += 1
        if _i in TRED_AT:
            _dv += 1
            M[("tred", TRED_AT[_i])] = _dv
        if _i % 16 == 15:
            _dv += 8
    for i in range(24):
        def mk(i=i):
            @blk.scalar
            def _(s):
                if i >= 2:
                    s.wait_ge(dv.h, M[("tred", i - 2)])
                src, dst = tern_ld_aps(i)
                dQ.dma(s.dma_start(out=dst, in_=src))
            M[("tld", i)] = dQ.n
        mk()

    @blk.vector
    def _(v):
        v.wait_ge(dS.h, M["x_ld"])
        for c in (8, 9, 10, 11):
            v.tensor_copy(xbf_s[:, 1024 * c:1024 * (c + 1)],
                          xt_s[:, 1024 * c:1024 * (c + 1)])
        dv.inc(v.drain())
    M["xbf_a"] = dv.n

    @blk.vector
    def _(v):
        for c in (12, 13, 14, 15):
            v.tensor_copy(xbf_s[:, 1024 * c:1024 * (c + 1)],
                          xt_s[:, 1024 * c:1024 * (c + 1)])
        dv.inc(v.drain())
    M["xbf_b"] = dv.n

    @blk.vector
    def _(v):
        for c in range(8):
            v.tensor_copy(xbf_s[:, 1024 * c:1024 * (c + 1)],
                          xt_s[:, 1024 * c:1024 * (c + 1)])
        dv.inc(v.drain())
    M["xbf"] = dv.n

    def w_slab_load(s, oc, fslot):
        srcw = wqkvT[:, 128 * oc:128 * (oc + 1)].rearrange(
            "(c p) o -> p c o", p=128)
        dS.dma(s.dma_start(
            out=wqf32[:, 2048 * fslot:2048 * (fslot + 1)].rearrange(
                "p (c o) -> p c o", o=128), in_=srcw))

    def v_tile_load(s, i):
        ov, c = divmod(i, 16)
        srcw = wqkvT[128 * c:128 * (c + 1),
                     2 * D + 512 * ov:2 * D + 512 * (ov + 1)]
        dS.dma(s.dma_start(out=vf32[:, 512 * (i % 8):512 * (i % 8) + 512],
                           in_=srcw))

    def w_slab_cast(v, slot, fslot):
        v.tensor_copy(wqb[:, D * slot:D * slot + D],
                      wqf32[:, 2048 * fslot:2048 * (fslot + 1)])
        dv.inc(v.drain())

    for j in range(2):
        def mk(j=j):
            @blk.sync
            def _(s):
                s.wait_ge(dv.h, M["xbf_a"])
                w_slab_load(s, 16 + j, j % 2)
            M[("wkl", j)] = dS.n

            if j == 0:
                @blk.vector
                def _(v):
                    v.wait_ge(dS.h, M[("wkl", j)])
                    w_slab_cast(v, j % 2, j % 2)
                M[("wk", j)] = dv.n
        mk()

    for j in range(16):
        def mk(j=j):
            @blk.tensor
            def _(t):
                t.wait_ge(dv.h, M[("wk", j)])
                if j == 0:
                    t.wait_ge(dv.h, M["xbf"])
                if j >= 2:
                    t.wait_ge(dv.h, M[("kev", j - 2)])
                for half in range(2):
                    b = 2 * (j % 2) + half
                    for c in range(HC):
                        inst = t.matmul(
                            bank(b),
                            wqb[:, D * (j % 2) + 128 * c:D * (j % 2) + 128 * (c + 1)],
                            xbf_s[:, 1024 * c + 512 * half:1024 * c + 512 * (half + 1)],
                            start=(c == 0), stop=(c == HC - 1))
                pe.inc(t.drain())
            M[("kmm", j)] = pe.n

            @blk.vector
            def _(v):
                v.wait_ge(pe.h, M[("kmm", j)])
                if j >= 2:
                    v.wait_ge(dS.h, M[("kst", j - 2)])
                for half in range(2):
                    inst = v.tensor_scalar(
                        kstage[:, 1024 * (j % 2) + 512 * half:
                               1024 * (j % 2) + 512 * (half + 1)],
                        bank(2 * (j % 2) + half), bk_s[:, j:j + 1], None,
                        op0=ALU.add)
                dv.inc(v.drain())
            M[("kev", j)] = dv.n

            @blk.sync
            def _(s):
                s.wait_ge(dv.h, M[("kev", j)])
                dS.dma(s.dma_start(
                    out=kvKV_in[128 * j:128 * (j + 1), :],
                    in_=kstage[:, 1024 * (j % 2):1024 * (j % 2) + 1024]))
            M[("kst", j)] = dS.n

            if j + 2 < 16:
                @blk.sync
                def _(s):
                    s.wait_ge(dv.h, M[("wk", j)])
                    w_slab_load(s, 16 + j + 2, j % 2)
                M[("wkl", j + 2)] = dS.n
            if j >= 8:
                @blk.sync
                def _(s):
                    if j == 8:
                        s.wait_ge(dv.h, M["xbf_b"])
                    v_tile_load(s, j - 8)
                M[("wvl", j - 8)] = dS.n
            if j + 1 < 16:
                @blk.vector
                def _(v):
                    v.wait_ge(dS.h, M[("wkl", j + 1)])
                    if j + 1 >= 2:
                        v.wait_ge(pe.h, M[("kmm", j - 1)])
                    w_slab_cast(v, (j + 1) % 2, (j + 1) % 2)
                M[("wk", j + 1)] = dv.n
        mk()

    for i in range(1):
        def mk(i=i):
            @blk.vector
            def _(v):
                v.wait_ge(dS.h, M[("wvl", i)])
                v.wait_ge(pe.h, M[("kmm", 15)])
                v.tensor_copy(wvb[:, 512 * (i % 2):512 * (i % 2) + 512],
                              vf32[:, 512 * (i % 8):512 * (i % 8) + 512])
                dv.inc(v.drain())
            M[("wv", i)] = dv.n
        mk()

    for ov in range(4):
        for c in range(16):
            i = 16 * ov + c

            def mk(i=i, ov=ov, c=c):
                @blk.tensor
                def _(t):
                    t.wait_ge(dv.h, M[("wv", i)])
                    if c == 0:
                        t.wait_ge(dv.h, M[("kev", 15)] if ov == 0
                                  else M[("vev", ov - 1, 7)])
                    for tc in range(8):
                        inst = t.matmul(
                            bank(tc),
                            xbf_s[:, 1024 * c + 128 * tc:1024 * c + 128 * (tc + 1)],
                            wvb[:, 512 * (i % 2):512 * (i % 2) + 512],
                            start=(c == 0), stop=(c == 15))
                    pe.inc(t.drain())
                M[("vmm", i)] = pe.n

                if i + 8 < 64:
                    @blk.sync
                    def _(s):
                        s.wait_ge(dv.h, M[("wv", i)])
                        v_tile_load(s, i + 8)
                    M[("wvl", i + 8)] = dS.n
                if i in (16, 17):
                    @blk.sync
                    def _(s):
                        s.wait_ge(dv.h, M[("wk", 14 + (i - 16))])
                        w_slab_load(s, i - 16, (i - 16) % 2)
                    M[("wql", i - 16)] = dS.n
                if i + 1 < 64:
                    @blk.vector
                    def _(v):
                        v.wait_ge(dS.h, M[("wvl", i + 1)])
                        if i + 1 >= 2:
                            v.wait_ge(pe.h, M[("vmm", i - 1)])
                        v.tensor_copy(
                            wvb[:, 512 * ((i + 1) % 2):512 * ((i + 1) % 2) + 512],
                            vf32[:, 512 * ((i + 1) % 8):
                                 512 * ((i + 1) % 8) + 512])
                        dv.inc(v.drain())
                    M[("wv", i + 1)] = dv.n
                if i in TRED_AT:
                    k = TRED_AT[i]

                    @blk.vector
                    def _(v):
                        v.wait_ge(dQ.h, M[("tld", k)])
                        v.tensor_reduce(
                            parts[:, k:k + 1],
                            ternb[:, 2048 * (k % 2):2048 * (k % 2) + 2048],
                            axis=AX.X, op=ALU.add, apply_absolute_value=True)
                        dv.inc(v.drain())
                    assert M[("tred", k)] == dv.n, (k, M[("tred", k)], dv.n)
            mk()

        for tc in range(8):
            def mk(ov=ov, tc=tc):
                j = 8 * ov + tc

                @blk.vector
                def _(v):
                    v.wait_ge(pe.h, M[("vmm", 16 * ov + 15)])
                    if j >= 2:
                        v.wait_ge(dS.h, M[("vst", j - 2)])
                    v.tensor_tensor(
                        vstage[:, 512 * (j % 2):512 * (j % 2) + 512], bank(tc),
                        bvbc[:, 512 * ov:512 * (ov + 1)], op=ALU.add)
                    dv.inc(v.drain())
                M[("vev", ov, tc)] = dv.n

                @blk.sync
                def _(s):
                    s.wait_ge(dv.h, M[("vev", ov, tc)])
                    kvv = kvKV_in[D:2 * D, :].rearrange("(t two) c -> t (two c)", two=2)
                    dS.dma(s.dma_start(
                        out=kvv[128 * tc:128 * (tc + 1), 512 * ov:512 * (ov + 1)],
                        in_=vstage[:, 512 * (j % 2):512 * (j % 2) + 512]))
                M[("vst", j)] = dS.n
            mk()

    @blk.tensor
    def _(t):
        t.wait_ge(dv.h, M[("tred", 23)])
        t.wait_ge(dv.h, M[("vev", 3, 7)])
        pe.inc(t.matmul(PP[0:1, 3584:3608], ones_f[:, 0:1], parts[:, 0:24],
                        start=True, stop=True))
    M["psum_mm"] = pe.n

    @blk.vector
    def _(v):
        v.wait_ge(pe.h, M["psum_mm"])
        v.tensor_reduce(arst[0:1, 0:3],
                        PP[0:1, 3584:3608].rearrange("a (m s) -> a m s", s=8),
                        axis=AX.X, op=ALU.add)
        dv.inc(v.drain())
    M["ar_ready"] = dv.n

    @blk.scalar
    def _(s):
        s.wait_ge(dv.h, M["ar_ready"])
        dQ.dma(s.dma_start(out=ar_in[:, :], in_=arst[0:1, 0:8]))
    M["ar_st"] = dQ.n

    @blk.gpsimd
    def _(g):
        g.wait_ge(dQ.h, M["ar_st"])
        cc.inc(g.collective_compute(
            "AllReduce", ALU.add, replica_groups=[list(range(N_CORES))],
            ins=[ar_in[:, :]], outs=[ar_out[:, :]]))
    M["ar_cc"] = cc.n

    for j in range(1):
        def mk(j=j):
            @blk.vector
            def _(v):
                v.wait_ge(dS.h, M[("wql", j)])
                v.wait_ge(pe.h, M[("vmm", 63)])
                w_slab_cast(v, j % 2, j % 2)
            M[("wq", j)] = dv.n
        mk()

    for j in range(16):
        def mk(j=j):
            @blk.tensor
            def _(t):
                t.wait_ge(dv.h, M[("wq", j)])
                if j < 2:
                    t.wait_ge(dv.h, M[("vev", 3, 7)])
                if j >= 2:
                    t.wait_ge(dv.h, M[("qev", j - 2)])
                for half in range(2):
                    b = 2 * (j % 2) + half
                    for c in range(HC):
                        inst = t.matmul(
                            bank(b),
                            wqb[:, D * (j % 2) + 128 * c:D * (j % 2) + 128 * (c + 1)],
                            xbf_s[:, 1024 * c + 512 * half:1024 * c + 512 * (half + 1)],
                            start=(c == 0), stop=(c == HC - 1))
                pe.inc(t.drain())
            M[("qmm", j)] = pe.n

            @blk.vector
            def _(v):
                v.wait_ge(pe.h, M[("qmm", j)])
                for half in range(2):
                    inst = v.tensor_scalar(
                        qT_s[:, 1024 * j + 512 * half:1024 * j + 512 * (half + 1)],
                        bank(2 * (j % 2) + half), bq_s[:, j:j + 1], None,
                        op0=ALU.add)
                dv.inc(v.drain())
            M[("qev", j)] = dv.n

            if j + 2 < 16:
                @blk.sync
                def _(s):
                    s.wait_ge(dv.h, M[("wq", j)])
                    w_slab_load(s, j + 2, j % 2)
                M[("wql", j + 2)] = dS.n
            if j + 1 < 16:
                @blk.vector
                def _(v):
                    v.wait_ge(dS.h, M[("wql", j + 1)])
                    if j + 1 >= 2:
                        v.wait_ge(pe.h, M[("qmm", j - 1)])
                    w_slab_cast(v, (j + 1) % 2, (j + 1) % 2)
                M[("wq", j + 1)] = dv.n
        mk()

    @blk.scalar
    def _(s):
        s.wait_ge(cc.h, M["ar_cc"])
        dQ.dma(s.dma_start(out=arst[0:1, 8:16], in_=ar_out[:, :]))
    M["ar_ld"] = dQ.n

    @blk.tensor
    def _(t):
        t.wait_ge(dQ.h, M["ar_ld"])
        pe.inc(t.matmul(PP[:, 3584:3592], ones_f[0:1, 0:128], arst[0:1, 8:16],
                        start=True, stop=True))
    M["ar_bc"] = pe.n

    @blk.vector
    def _(v):
        v.wait_ge(pe.h, M["ar_bc"])
        v.tensor_scalar(svec[:], PP[:, 3584:3592], WN_INV, LN_EPS,
                        op0=ALU.mult, op1=ALU.add)
        v.drain()
        v.reciprocal(ivec[:], svec[:])
        dv.inc(v.drain())
    M["ivec"] = dv.n

    for i in range(24):
        def mk(i=i):
            @blk.scalar
            def _(s):
                if i == 0:
                    s.wait_ge(dv.h, M[("tred", 23)])
                if i >= 2:
                    s.wait_ge(dv.h, M[("tq", i - 2)])
                src, dst = tern_ld_aps(i)
                dQ.dma(s.dma_start(out=dst, in_=src))
            M[("tld2", i)] = dQ.n

            @blk.vector
            def _(v):
                v.wait_ge(dQ.h, M[("tld2", i)])
                if i == 0:
                    v.wait_ge(dv.h, M["ivec"])
                if i >= 2:
                    v.wait_ge(dQ.h, M[("tst", i - 2)])
                m = i // 8
                buf = ternb[:, 2048 * (i % 2):2048 * (i % 2) + 2048]
                v.tensor_scalar(buf, buf, ivec[:, m:m + 1], -CLIP,
                                op0=ALU.mult, op1=ALU.max)
                v.tensor_scalar(
                    terni[:, 2048 * (i % 2):2048 * (i % 2) + 2048], buf,
                    CLIP, None, op0=ALU.min)
                dv.inc(v.drain())
            M[("tq", i)] = dv.n

            @blk.scalar
            def _(s):
                s.wait_ge(dv.h, M[("tq", i)])
                dst, src = tern_st_aps(i)
                dQ.dma(s.dma_start(out=dst, in_=src))
            M[("tst", i)] = dQ.n
        mk()


    @blk.gpsimd
    def _(g):
        g.wait_ge(dS.h, M[("vst", 31)])
        g.wait_ge(cc.h, M["ar_cc"])
        g.wait_ge(dv.h, M["ivec"])
        cc.inc(g.collective_compute(
            "AllGather", ALU.bypass, replica_groups=[list(range(N_CORES))],
            ins=[kvKV_in[:, :]], outs=[kvKV_out[:, :]]))
    M["ag_k"] = cc.n
    M["ag_v"] = cc.n

    for i in range(2):
        def mk(i=i):
            @blk.gpsimd
            def _(g):
                g.wait_ge(cc.h, M["ag_v"])
                rp, sub = divmod(i % 16, 2)
                src = kvKV_out[2 * D * rp + D:2 * D * (rp + 1), :].rearrange(
                    "(t two) c -> t (two c)", two=2)[512 * sub:512 * (sub + 1), :]
                dW.dma(g.dma_start(
                    out=vbuf[:, 8192 * (i % 2):8192 * (i % 2) + 8192].rearrange(
                        "p (n c) -> p n c", c=D),
                    in_=src.rearrange("(n p) c -> p n c", p=128)))
            M[("vld", i)] = dW.n
        mk()

    if "qt" in dbg:
        @blk.sync
        def _(s):
            s.wait_ge(dv.h, M[("qev", 15)])
            dS.dma(s.dma_start(out=dbg["qt"][:, :], in_=qT_s[:]))
    if "kv" in dbg:
        @blk.sync
        def _(s):
            s.wait_ge(cc.h, M["ag_v"])
            dS.dma(s.dma_start(out=dbg["kv"][:, :], in_=kvKV_out[:, :]))

    for i in range(2):
        def mk(i=i):
            @blk.scalar
            def _(s):
                s.wait_ge(cc.h, M["ag_v"])
                if i < 2:
                    s.wait_ge(dv.h, M["xbf"])
                rp, sub = divmod(i % 16, 2)
                src = kvKV_out[2 * D * rp:2 * D * rp + D,
                              512 * sub:512 * (sub + 1)].rearrange(
                    "(c p) w -> p c w", p=128)
                dQ.dma(s.dma_start(
                    out=kbuf[:, 8192 * (i % 2):8192 * (i % 2) + 8192].rearrange(
                        "p (c w) -> p c w", w=512), in_=src))
            M[("kld", i)] = dQ.n
        mk()

    for qb in range(NQB):
        for kb in range(NKB):
            i = 16 * qb + kb

            def mk(i=i, qb=qb, kb=kb):
                @blk.tensor
                def _(t):
                    t.wait_ge(dQ.h, M[("kld", i)])
                    if i < 2:
                        t.wait_ge(dv.h, M[("qev", 15)])
                    if i >= 2:
                        t.wait_ge(dv.h, M[("ev", i - 2)])
                    for c in range(HC):
                        inst = t.matmul(
                            bank(i % 2),
                            qT_s[:, 1024 * c + 128 * qb:1024 * c + 128 * (qb + 1)],
                            kbuf[:, 8192 * (i % 2) + 512 * c:
                                 8192 * (i % 2) + 512 * (c + 1)],
                            start=(c == 0), stop=(c == HC - 1))
                    pe.inc(inst)
                M[("p1", i)] = pe.n

                @blk.vector
                def _(v):
                    v.wait_ge(pe.h, M[("p1", i)])
                    if qb > 0 and kb == 0:
                        v.wait_ge(ac.h, M[("id", qb - 1)])
                    v.tensor_scalar(S_s[:, 512 * kb:512 * (kb + 1)],
                                    bank(i % 2), C_SCALE, None,
                                    op0=ALU.mult)
                    inst = v.tensor_reduce(mb_t[:, kb:kb + 1], bank(i % 2),
                                           axis=AX.X, op=ALU.max)
                    dv.inc(inst)
                M[("ev", i)] = dv.n

                if i + 2 < NI:
                    @blk.scalar
                    def _(s):
                        j = i + 2
                        s.wait_ge(pe.h, M[("p1", i)])
                        rp, sub = divmod(j % 16, 2)
                        src = kvKV_out[2 * D * rp:2 * D * rp + D,
                                      512 * sub:512 * (sub + 1)].rearrange(
                            "(c p) w -> p c w", p=128)
                        dQ.dma(s.dma_start(
                            out=kbuf[:, 8192 * (j % 2):8192 * (j % 2) + 8192]
                            .rearrange("p (c w) -> p c w", w=512), in_=src))
                    M[("kld", i + 2)] = dQ.n
            mk()

        def mkq(qb=qb):
            @blk.vector
            def _(v):
                v.wait_ge(dv.h, M[("ev", 16 * qb + 15)])
                v.drain()
                v.tensor_reduce(mneg[:], mb_t[:], axis=AX.X, op=ALU.max,
                                negate=True)
                v.drain()
                v.tensor_scalar(mneg[:], mneg[:], C_SCALE, None,
                                op0=ALU.mult)
                dv.inc(v.drain())
            M[("mx", qb)] = dv.n
        mkq()

        for kb in range(NKB):
            i = 16 * qb + kb

            def mk(i=i, qb=qb, kb=kb):
                @blk.scalar
                def _(s):
                    s.wait_ge(dv.h, M[("mx", qb)])
                    if i >= 2:
                        s.wait_ge(dS.h, M[("tr", i - 2)])
                    s.activation(p_bf[:, 512 * (i % 2):512 * (i % 2) + 512],
                                 S_s[:, 512 * kb:512 * (kb + 1)],
                                 AF.Exp, bias=mneg[:])
                    ac.inc(s.drain())
                M[("exp", i)] = ac.n

                @blk.sync
                def _(s):
                    s.wait_ge(ac.h, M[("exp", i)])
                    if i >= 2:
                        s.wait_ge(p2.h, M[("p2", i - 2)])
                    for j in range(4):
                        dS.dma(s.dma_start(
                            out=pT_b[:, 512 * (i % 2) + 128 * j:
                                     512 * (i % 2) + 128 * (j + 1)],
                            in_=p_bf[:, 512 * (i % 2) + 128 * j:
                                     512 * (i % 2) + 128 * (j + 1)],
                            transpose=True))
                M[("tr", i)] = dS.n

                @blk.tensor
                def _(t):
                    t.wait_ge(dS.h, M[("tr", i)])
                    t.wait_ge(dW.h, M[("vld", i)])
                    if kb == 0 and qb > 0:
                        t.wait_ge(ac.h, M[("id", qb - 1)])
                    for tkc in range(4):
                        lhs = pT_b[:, 512 * (i % 2) + 128 * tkc:
                                   512 * (i % 2) + 128 * (tkc + 1)]
                        first = (kb == 0 and tkc == 0)
                        last = (kb == 15 and tkc == 3)
                        for hb in range(4):
                            t.matmul(
                                PP[:, 1024 + 512 * hb:1536 + 512 * hb], lhs,
                                vbuf[:, 8192 * (i % 2) + 2048 * tkc + 512 * hb:
                                     8192 * (i % 2) + 2048 * tkc + 512 * (hb + 1)],
                                start=first, stop=last)
                        inst = t.matmul(PP[:, 3072:3073], lhs, ones_b[:, 0:1],
                                        start=first, stop=last)
                    p2.inc(inst)
                M[("p2", i)] = p2.n

                if i + 2 < NI:
                    @blk.gpsimd
                    def _(g):
                        j = i + 2
                        g.wait_ge(p2.h, M[("p2", i)])
                        rp, sub = divmod(j % 16, 2)
                        src = kvKV_out[2 * D * rp + D:2 * D * (rp + 1), :].rearrange(
                            "(t two) c -> t (two c)", two=2)[
                            512 * sub:512 * (sub + 1), :]
                        dW.dma(g.dma_start(
                            out=vbuf[:, 8192 * (j % 2):8192 * (j % 2) + 8192]
                            .rearrange("p (n c) -> p n c", c=D),
                            in_=src.rearrange("(n p) c -> p n c", p=128)))
                    M[("vld", i + 2)] = dW.n
            mk()

        def mkq2(qb=qb):
            @blk.vector
            def _(v):
                v.wait_ge(p2.h, M[("p2", 16 * qb + 15)])
                if qb > 0:
                    v.wait_ge(ac.h, M[("id", qb - 1)])
                v.reciprocal(linv[:], PP[:, 3072:3073])
                dv.inc(v.drain())
            M[("lv", qb)] = dv.n

            if qb == 0 and "m0" in dbg:
                @blk.vector
                def _(v):
                    v.tensor_copy(tmp1k[:, 0:1], mneg[:])
                    v.tensor_copy(tmp1k[:, 1:2], linv[:])
                    v.tensor_copy(tmp1k[:, 2:3], PP[:, 3072:3073])
                    v.drain()
                    v.tensor_copy(tmp1k[:, 3:16], mb_t[:, 0:13])
                    dv.inc(v.drain())
                M["m0d"] = dv.n

                @blk.sync
                def _(s):
                    s.wait_ge(dv.h, M["m0d"])
                    dS.dma(s.dma_start(out=dbg["m0"][:, :], in_=tmp1k[:, 0:16]))
                M["m0st"] = dS.n

            @blk.scalar
            def _(s):
                s.wait_ge(dv.h, M[("lv", qb)])
                if qb > 0:
                    s.wait_ge(dS.h, M[("ctr", qb - 1)])
                s.activation(ctxbf[:], PP[:, 1024:3072], AF.Identity,
                             scale=linv[:])
                ac.inc(s.drain())
            M[("id", qb)] = ac.n

            if qb < 7:
                @blk.sync
                def _(s):
                    s.wait_ge(ac.h, M[("id", qb)])
                    for c in range(HC):
                        dS.dma(s.dma_start(
                            out=ctxT_s[:, 1024 * c + 128 * qb:
                                       1024 * c + 128 * (qb + 1)],
                            in_=ctxbf[:, 128 * c:128 * (c + 1)], transpose=True))
                M[("ctr", qb)] = dS.n
            else:
                @blk.scalar
                def _(s):
                    s.wait_ge(ac.h, M[("id", qb)])
                    for c in range(HC):
                        dQ.dma(s.dma_start(
                            out=ctxT_s[:, 1024 * c + 128 * qb:
                                       1024 * c + 128 * (qb + 1)],
                            in_=ctxbf[:, 128 * c:128 * (c + 1)], transpose=True))
                M[("ctr", qb)] = dQ.n
        mkq2()
        if qb == 0 and "s0" in dbg:
            @blk.sync
            def _(s):
                s.wait_ge(ac.h, M[("exp", 15)])
                dS.dma(s.dma_start(out=dbg["s0"][:, :], in_=S_s[:]))

    if "s" in dbg:
        @blk.sync
        def _(s):
            s.wait_ge(ac.h, M[("id", NQB - 1)])
            dS.dma(s.dma_start(out=dbg["s"][:, :], in_=S_s[:]))
    if "ctxT" in dbg:
        @blk.sync
        def _(s):
            s.wait_ge(dQ.h, M[("ctr", NQB - 1)])
            dS.dma(s.dma_start(out=dbg["ctxT"][:, :], in_=ctxT_s[:]))

    @blk.gpsimd
    def _(g):
        g.wait_ge(dQ.h, M[("tst", 23)])
        g.wait_ge(cc.h, M["ag_v"])
        g.wait_ge(ac.h, M[("id", NQB - 1)])
        cc.inc(g.collective_compute(
            "AllGather", ALU.bypass, replica_groups=[list(range(N_CORES))],
            ins=[tern_in[:, :]], outs=[tern_out[:, :]]))
    M["tern_ag"] = cc.n
    if "tern" in dbg:
        @blk.sync
        def _(s):
            s.wait_ge(cc.h, M["tern_ag"])
            dS.dma(s.dma_start(out=dbg["tern"][:, :], in_=tern_out[:, :]))

    @blk.sync
    def _(s):
        s.wait_ge(pe.h, M[("p1", NI - 1)])
        dS.dma(s.dma_start(
            out=xt_s[:, 0:8192].rearrange("p (c t) -> p c t", t=1024),
            in_=xT[0:1024, :].rearrange("(c p) t -> p c t", p=128)))
    M["x_reA"] = dS.n

    @blk.sync
    def _(s):
        s.wait_ge(p2.h, M[("p2", NI - 1)])
        dS.dma(s.dma_start(
            out=xt_s[:, 8192:16384].rearrange("p (c t) -> p c t", t=1024),
            in_=xT[1024:2048, :].rearrange("(c p) t -> p c t", p=128)))
    M["x_re"] = dS.n

    def wo_slab_load(s, oc):
        srcw = woutT[:, 128 * oc:128 * (oc + 1)].rearrange(
            "(c p) o -> p c o", p=128)
        dS.dma(s.dma_start(
            out=wof32[:].rearrange("p (c o) -> p c o", o=128), in_=srcw))

    for oc in range(1):
        def mk(oc=oc):
            @blk.sync
            def _(s):
                s.wait_ge(pe.h, M[("p1", NI - 1)])
                wo_slab_load(s, oc)
            M[("wol", oc)] = dS.n

            @blk.vector
            def _(v):
                v.wait_ge(dS.h, M[("wol", oc)])
                v.tensor_copy(wob[:, D * (oc % 2):D * (oc % 2) + D], wof32[:])
                dv.inc(v.drain())
            M[("wo", oc)] = dv.n
        mk()

    for oc in range(HC):
        def mk(oc=oc):
            @blk.tensor
            def _(t):
                t.wait_ge(dv.h, M[("wo", oc)])
                if oc == 0:
                    t.wait_ge(dQ.h, M[("ctr", NQB - 1)])
                if oc >= 2:
                    t.wait_ge(dv.h, M[("stt", oc - 2)])
                for half in range(2):
                    b = 2 * (oc % 2) + half
                    for c in range(HC):
                        inst = t.matmul(
                            bank(b),
                            wob[:, D * (oc % 2) + 128 * c:D * (oc % 2) + 128 * (c + 1)],
                            ctxT_s[:, 1024 * c + 512 * half:
                                   1024 * c + 512 * (half + 1)],
                            start=(c == 0), stop=(c == HC - 1))
                pe.inc(inst)
            M[("op", oc)] = pe.n

            if oc + 1 < HC:
                @blk.sync
                def _(s):
                    s.wait_ge(dv.h, M[("wo", oc)])
                    wo_slab_load(s, oc + 1)
                M[("wol", oc + 1)] = dS.n

                @blk.vector
                def _(v):
                    v.wait_ge(dS.h, M[("wol", oc + 1)])
                    if oc + 1 >= 2:
                        v.wait_ge(pe.h, M[("op", oc - 1)])
                    v.tensor_copy(wob[:, D * ((oc + 1) % 2):
                                      D * ((oc + 1) % 2) + D], wof32[:])
                    dv.inc(v.drain())
                M[("wo", oc + 1)] = dv.n

            @blk.vector
            def _(v):
                v.wait_ge(pe.h, M[("op", oc)])
                v.wait_ge(dS.h, M["x_reA"] if oc < 8 else M["x_re"])
                if oc >= 2:
                    v.wait_ge(pe.h, M[("st1", oc - 2)])
                for half in range(2):
                    sl = slice(1024 * oc + 512 * half,
                               1024 * oc + 512 * (half + 1))
                    v.scalar_tensor_tensor(xt_s[:, sl], bank(2 * (oc % 2) + half),
                                           bo_s[:, oc:oc + 1], xt_s[:, sl],
                                           op0=ALU.add, op1=ALU.add)
                v.tensor_copy(rbf_t[:, 1024 * (oc % 2):1024 * (oc % 2) + 1024],
                              xt_s[:, 1024 * oc:1024 * (oc + 1)])
                dv.inc(v.drain())
            M[("stt", oc)] = dv.n

            @blk.scalar
            def _(s):
                s.wait_ge(dv.h, M[("stt", oc)])
                s.activation(r2bf_t[:, 1024 * (oc % 2):1024 * (oc % 2) + 1024],
                             xt_s[:, 1024 * oc:1024 * (oc + 1)], AF.Square)
                ac.inc(s.drain())
            M[("sq", oc)] = ac.n

            @blk.tensor
            def _(t):
                t.wait_ge(ac.h, M[("sq", oc)])
                if oc == 0:
                    t.wait_ge(dv.h, M["ivec"])
                ro = 1024 * (oc % 2)
                for half in range(2):
                    t.matmul(bank(4 + half), ones_b[:],
                             rbf_t[:, ro + 512 * half:ro + 512 * (half + 1)],
                             start=(oc == 0), stop=(oc == HC - 1))
                    inst = t.matmul(bank(6 + half), ones_b[:],
                                    r2bf_t[:, ro + 512 * half:ro + 512 * (half + 1)],
                                    start=(oc == 0), stop=(oc == HC - 1))
                pe.inc(inst)
            M[("st1", oc)] = pe.n
        mk()

    @blk.vector
    def _(v):
        v.wait_ge(pe.h, M[("st1", HC - 1)])
        v.tensor_scalar(tmp1k[:], PP[:, 2048:3072], INV_D, None, op0=ALU.mult)
        v.tensor_scalar(a1_t[:], PP[:, 3072:4096], INV_D, None, op0=ALU.mult)
        v.tensor_tensor(c1_t[:], tmp1k[:], tmp1k[:], op=ALU.mult)
        v.tensor_tensor(a1_t[:], a1_t[:], c1_t[:], op=ALU.subtract)
        v.tensor_scalar(a1_t[:], a1_t[:], LN_EPS, None, op0=ALU.add)
        v.reciprocal(c1_t[:], a1_t[:])
        dv.inc(v.drain())
    M["ln1a"] = dv.n

    @blk.scalar
    def _(s):
        s.wait_ge(dv.h, M["ln1a"])
        s.activation(a1_t[:], c1_t[:], AF.Sqrt)
        ac.inc(s.drain())
    M["ln1b"] = ac.n

    @blk.vector
    def _(v):
        v.wait_ge(ac.h, M["ln1b"])
        v.tensor_tensor(c1_t[:], tmp1k[:], a1_t[:], op=ALU.mult)
        dv.inc(v.drain())
    M["ln1c"] = dv.n

    for oc in range(HC):
        def mk(oc=oc):
            @blk.vector
            def _(v):
                if oc == 0:
                    v.wait_ge(dv.h, M["ln1c"])
                sl = slice(1024 * oc, 1024 * (oc + 1))
                v.tensor_tensor(tmp1k[:], xt_s[:, sl], a1_t[:], op=ALU.mult)
                v.tensor_tensor(xt_s[:, sl], tmp1k[:], c1_t[:], op=ALU.subtract)
                if oc == 0:
                    v.tensor_tensor(acc_t[:], xt_s[:, sl], xt_s[:, sl],
                                    op=ALU.max)
                    inst = v.tensor_tensor(acc2_t[:], xt_s[:, sl], xt_s[:, sl],
                                           op=ALU.min)
                else:
                    v.tensor_tensor(acc_t[:], acc_t[:], xt_s[:, sl],
                                    op=ALU.max)
                    inst = v.tensor_tensor(acc2_t[:], acc2_t[:], xt_s[:, sl],
                                           op=ALU.min)
                dv.inc(v.drain() if oc == HC - 1 else inst)
            M[("ap1", oc)] = dv.n
        mk()

    @blk.sync
    def _(s):
        s.wait_ge(dv.h, M[("ap1", HC - 1)])
        dS.dma(s.dma_start(out=x1_sc.rearrange("(c p) t -> p c t", p=128),
                           in_=xt_s[:].rearrange("p (c t) -> p c t", t=1024)))
    M["x1_sp"] = dS.n
    if "x1" in dbg:
        @blk.sync
        def _(s):
            s.wait_ge(dv.h, M[("ap1", HC - 1)])
            dS.dma(s.dma_start(out=dbg["x1"][:, :], in_=xt_s[:]))

    def quant_scales(tag, nblk, acc_ap, dv_ready, sx_out, ix_out):
        for tb in range(nblk):
            def mk(tb=tb):
                @blk.tensor
                def _(t):
                    if tb == 0:
                        t.wait_ge(dv.h, dv_ready)
                    else:
                        t.wait_ge(dv.h, M[(tag, "red", tb - 1)])
                    pe.inc(t.matmul(PP[:, 3584:3712],
                                    acc_ap[:, 128 * tb:128 * (tb + 1)], ident[:],
                                    is_transpose=True, start=True, stop=True))
                M[(tag, "tp", tb)] = pe.n

                @blk.vector
                def _(v):
                    v.wait_ge(pe.h, M[(tag, "tp", tb)])
                    v.tensor_reduce(stok[:, tb:tb + 1], PP[:, 3584:3712],
                                    axis=AX.X, op=ALU.max)
                    dv.inc(v.drain())
                M[(tag, "red", tb)] = dv.n
            mk()

        @blk.vector
        def _(v):
            v.wait_ge(dv.h, M[(tag, "red", nblk - 1)])
            v.drain()
            v.tensor_scalar(stok[:, 8:8 + nblk], stok[:, 0:nblk], Q_EPS, INV127,
                            op0=ALU.max, op1=ALU.mult)
            v.drain()
            v.reciprocal(itok[:, 0:nblk], stok[:, 8:8 + nblk])
            dv.inc(v.drain())
        M[(tag, "tok")] = dv.n

        prevtp = None
        for tb in range(nblk):
            for which in ("s", "i"):
                tokap = stok[:, 8 + tb:9 + tb] if which == "s" else \
                    itok[:, tb:tb + 1]
                dst = sx_out if which == "s" else ix_out

                def mk(tb=tb, which=which, tokap=tokap, dst=dst, prevtp=prevtp):
                    @blk.vector
                    def _(v):
                        if prevtp is None:
                            v.wait_ge(dv.h, M[(tag, "tok")])
                        else:
                            v.wait_ge(pe.h, prevtp)
                        v.tensor_scalar(btile[:], ones_f[:], tokap, None,
                                        op0=ALU.mult)
                        dv.inc(v.drain())
                    M[(tag, "bt", tb, which)] = dv.n

                    @blk.tensor
                    def _(t):
                        t.wait_ge(dv.h, M[(tag, "bt", tb, which)])
                        pe.inc(t.matmul(PP[:, 3584:3712], btile[:], ident[:],
                                        is_transpose=True, start=True, stop=True))
                    M[(tag, "btp", tb, which)] = pe.n

                    @blk.vector
                    def _(v):
                        v.wait_ge(pe.h, M[(tag, "btp", tb, which)])
                        v.tensor_copy(dst[:, 128 * tb:128 * (tb + 1)],
                                      PP[:, 3584:3712])
                        dv.inc(v.drain())
                    M[(tag, "bc", tb, which)] = dv.n
                mk()
                prevtp = M[(tag, "btp", tb, which)]
        M[(tag, "done")] = dv.n

    @blk.vector
    def _(v):
        v.wait_ge(dv.h, M[("ap1", HC - 1)])
        v.wait_ge(cc.h, M["tern_ag"])
        v.tensor_scalar(acc2_t[:], acc2_t[:], -1.0, None, op0=ALU.mult)
        v.tensor_tensor(acc_t[:], acc_t[:], acc2_t[:], op=ALU.max)
        dv.inc(v.drain())
    M["qx_acc"] = dv.n

    quant_scales("qx", 8, acc_t, M["qx_acc"], sxbc, ixbc)

    @blk.vector
    def _(v):
        v.wait_ge(dv.h, M[("qx", "done")])
        v.wait_ge(dQ.h, M[("tst", 23)])
        v.tensor_scalar(bg_t[:], sxbc[:], svec[:, 0:1], None, op0=ALU.mult)
        v.tensor_scalar(bu_t[:], sxbc[:], svec[:, 1:2], None, op0=ALU.mult)
        for oc in range(HC):
            sl = slice(1024 * oc, 1024 * (oc + 1))
            v.tensor_tensor(i8tmp[:], xt_s[:, sl], ixbc[:], op=ALU.mult)
            v.tensor_copy(xq_s[:, sl], i8tmp[:])
        dv.inc(v.drain())
    M["xq"] = dv.n
    if "xq" in dbg:
        @blk.sync
        def _(s):
            s.wait_ge(dv.h, M["xq"])
            dS.dma(s.dma_start(out=dbg["xq"][:, :], in_=xq_s[:]))

    whole = tern_out.rearrange("(r q) w -> r q w", q=1536)

    def gu_slab_aps(mat, fb, s1):
        two0, woff = divmod(256 * fb, 4096)
        r0 = 512 * mat + 256 * s1 + two0
        src = whole[:, r0:r0 + 256:2, woff:woff + 256].rearrange(
            "r p f -> p r f")
        return src

    wd_view = []
    for rr in range(8):
        wd_view.append(
            tern_out[1536 * rr + 1024:1536 * rr + 1536, :].rearrange(
                "r (par o) -> (r par) o", par=2))

    for half in range(2):
        hof = 512 * half
        for fb in range(2):
            def mk(fb=fb, half=half):
                @blk.gpsimd
                def _(g):
                    g.wait_ge(cc.h, M["tern_ag"])
                    if half == 0 and fb == 0:
                        g.wait_ge(pe.h, M[("op", HC - 1)])
                    if half == 1:
                        g.wait_ge(dv.h, M[("gu", 0, 63)])
                    for mat in range(2):
                        for s1 in range(2):
                            base = 8192 * (fb % 2) + 4096 * mat
                            dst = wgub[:, base:base + 4096].rearrange(
                                "p (r f2) -> p r f2", f2=512)[:, :, 256 * s1:
                                                             256 * (s1 + 1)]
                            dW.dma(g.dma_start(out=dst,
                                               in_=gu_slab_aps(mat, fb, s1)))
                M[("gsl", half, fb)] = dW.n
            mk()

        for f in range(64):
            fb, fc2 = divmod(f, 2)

            def mk(f=f, fb=fb, fc2=fc2, half=half, hof=hof):
                @blk.tensor
                def _(t):
                    t.wait_ge(dW.h, M[("gsl", half, fb)])
                    if half == 0 and f == 0:
                        t.wait_ge(dv.h, M["xq"])
                    if f >= 2:
                        t.wait_ge(dv.h, M[("gup", half, f - 2)])
                    elif half == 1:
                        t.wait_ge(dv.h, M[("gup", 0, 63 - (1 - f))])
                    for mat in range(2):
                        b = 2 * (f % 2) + mat
                        for c in range(HC):
                            base = 8192 * (fb % 2) + 4096 * mat
                            inst = t.matmul(
                                bank(b),
                                wgub[:, base + 256 * c + 128 * fc2:
                                     base + 256 * c + 128 * (fc2 + 1)],
                                xq_s[:, 1024 * c + hof:1024 * c + hof + 512],
                                start=(c == 0), stop=(c == HC - 1))
                    pe.inc(inst)
                M[("gumm", half, f)] = pe.n

                if fb + 2 < 32 and fc2 == 1:
                    @blk.gpsimd
                    def _(g):
                        g.wait_ge(pe.h, M[("gumm", half, f)])
                        for mat in range(2):
                            for s1 in range(2):
                                base = 8192 * (fb % 2) + 4096 * mat
                                dst = wgub[:, base:base + 4096].rearrange(
                                    "p (r f2) -> p r f2", f2=512)[
                                    :, :, 256 * s1:256 * (s1 + 1)]
                                dW.dma(g.dma_start(
                                    out=dst, in_=gu_slab_aps(mat, fb + 2, s1)))
                    M[("gsl", half, fb + 2)] = dW.n

                @blk.vector
                def _(v):
                    v.wait_ge(pe.h, M[("gumm", half, f)])
                    v.tensor_tensor(sgt[:], bank(2 * (f % 2)),
                                    bg_t[:, hof:hof + 512], op=ALU.mult)
                    v.tensor_tensor(upt[:], bank(2 * (f % 2) + 1),
                                    bu_t[:, hof:hof + 512], op=ALU.mult)
                    dv.inc(v.drain())
                M[("gup", half, f)] = dv.n

                @blk.scalar
                def _(s):
                    s.wait_ge(dv.h, M[("gup", half, f)])
                    s.activation(sgt[:], sgt[:], AF.Sigmoid)
                    ac.inc(s.drain())
                M[("sig", half, f)] = ac.n

                @blk.vector
                def _(v):
                    v.wait_ge(ac.h, M[("sig", half, f)])
                    if half == 0 and f == 0:
                        v.wait_ge(dS.h, M["x1_sp"])
                    sl = slice(512 * f, 512 * (f + 1))
                    v.tensor_tensor(h_s[:, sl], sgt[:], upt[:], op=ALU.mult)
                    if f == 0:
                        v.tensor_tensor(hacc[:], h_s[:, sl], h_s[:, sl],
                                        op=ALU.max)
                        inst = v.tensor_tensor(hacc2[:], h_s[:, sl], h_s[:, sl],
                                               op=ALU.min)
                    else:
                        v.tensor_tensor(hacc[:], hacc[:], h_s[:, sl],
                                        op=ALU.max)
                        inst = v.tensor_tensor(hacc2[:], hacc2[:], h_s[:, sl],
                                               op=ALU.min)
                    dv.inc(inst)
                M[("gu", half, f)] = dv.n
            mk()

        def mkhacc(half=half):
            @blk.vector
            def _(v):
                v.wait_ge(dv.h, M[("gu", half, 63)])
                v.tensor_scalar(hacc2[:], hacc2[:], -1.0, None, op0=ALU.mult)
                v.tensor_tensor(hacc[:], hacc[:], hacc2[:], op=ALU.max)
                dv.inc(v.drain())
            M[("qh_acc", half)] = dv.n
        mkhacc()

        quant_scales(("qh", half), 4, hacc, M[("qh_acc", half)], shb, ihb)

        def mkhq(half=half):
            @blk.vector
            def _(v):
                v.wait_ge(dv.h, M[(("qh", half), "done")])
                v.tensor_scalar(dsc[:], shb[:, 0:512], svec[:, 2:3], None,
                                op0=ALU.mult)
                for f in range(64):
                    sl = slice(512 * f, 512 * (f + 1))
                    v.tensor_tensor(i8tmp[:, 0:512], h_s[:, sl], ihb[:, 0:512],
                                    op=ALU.mult)
                    v.tensor_copy(h_s[:, sl], i8tmp[:, 0:512])
                dv.inc(v.drain())
            M[("hq", half)] = dv.n
        mkhq()
        if "h" in dbg and half == 0:
            @blk.sync
            def _(s):
                s.wait_ge(dv.h, M[("hq", 0)])
                dS.dma(s.dma_start(out=dbg["h"][:, :], in_=h_s[:]))

        for og in range(2):
            for f in range(2):
                def mk(f=f, og=og, half=half):
                    @blk.gpsimd
                    def _(g):
                        if og == 0 and f == 0:
                            g.wait_ge(dv.h, M[("hq", half)])
                        if og == 1 and f < 2:
                            g.wait_ge(pe.h, M[("dmm", half, 0, 63)])
                        rr, lf = divmod(f, 8)
                        dW.dma(g.dma_start(
                            out=wdb[:, 1024 * (f % 2):1024 * (f % 2) + 1024],
                            in_=wd_view[rr][128 * lf:128 * (lf + 1),
                                            1024 * og:1024 * (og + 1)]))
                    M[("wdl", half, og, f)] = dW.n
                mk()

            for f in range(64):
                def mk(f=f, og=og, half=half):
                    @blk.tensor
                    def _(t):
                        t.wait_ge(dW.h, M[("wdl", half, og, f)])
                        if og == 0 and f == 0:
                            t.wait_ge(dv.h, M[("hq", half)])
                        if og == 1 and f == 0:
                            t.wait_ge(dv.h, M[("dep", half, 0, 7)])
                        for oc8 in range(8):
                            inst = t.matmul(
                                bank(oc8),
                                wdb[:, 1024 * (f % 2) + 128 * oc8:
                                     1024 * (f % 2) + 128 * (oc8 + 1)],
                                h_s[:, 512 * f:512 * (f + 1)],
                                start=(f == 0), stop=(f == 63))
                        pe.inc(inst)
                    M[("dmm", half, og, f)] = pe.n

                    if f + 2 < 64:
                        @blk.gpsimd
                        def _(g):
                            g.wait_ge(pe.h, M[("dmm", half, og, f)])
                            rr, lf = divmod(f + 2, 8)
                            dW.dma(g.dma_start(
                                out=wdb[:, 1024 * (f % 2):1024 * (f % 2) + 1024],
                                in_=wd_view[rr][128 * lf:128 * (lf + 1),
                                                1024 * og:1024 * (og + 1)]))
                        M[("wdl", half, og, f + 2)] = dW.n
                mk()

            for oc8 in range(8):
                oc = 8 * og + oc8

                def mk(oc8=oc8, oc=oc, og=og, half=half, hof=hof):
                    @blk.sync
                    def _(s):
                        if og == 0 and oc8 == 0:
                            s.wait_ge(dv.h, M[("hq", half)])
                        else:
                            s.wait_ge(dv.h, M[("dep", half,
                                               og if oc8 else og - 1,
                                               (oc8 - 1) % 8)])
                        dS.dma(s.dma_start(
                            out=x1re[:],
                            in_=x1_sc[128 * oc:128 * (oc + 1), hof:hof + 512]))
                    M[("x1r", half, og, oc8)] = dS.n

                    @blk.vector
                    def _(v):
                        v.wait_ge(pe.h, M[("dmm", half, og, 63)])
                        v.wait_ge(dS.h, M[("x1r", half, og, oc8)])
                        v.tensor_tensor(tmp1k[:, 0:512], bank(oc8), dsc[:, 0:512],
                                        op=ALU.mult)
                        v.tensor_tensor(r2_s[:, 512 * oc:512 * (oc + 1)],
                                        tmp1k[:, 0:512], x1re[:],
                                        op=ALU.add)
                        dv.inc(v.drain())
                    M[("dep", half, og, oc8)] = dv.n
                mk()

        for oc in range(HC):
            def mk(oc=oc, half=half):
                @blk.vector
                def _(v):
                    if oc == 0:
                        v.wait_ge(dv.h, M[("dep", half, 1, 7)])
                    if oc >= 1:
                        v.wait_ge(pe.h, M[("st2", half, oc - 1)])
                    v.tensor_copy(rb2[:], r2_s[:, 512 * oc:512 * (oc + 1)])
                    dv.inc(v.drain())
                M[("rc2", half, oc)] = dv.n

                @blk.scalar
                def _(s):
                    s.wait_ge(dv.h, M[("rc2", half, oc)])
                    s.activation(r2b2[:], r2_s[:, 512 * oc:512 * (oc + 1)],
                                 AF.Square)
                    ac.inc(s.drain())
                M[("sq2", half, oc)] = ac.n

                @blk.tensor
                def _(t):
                    t.wait_ge(ac.h, M[("sq2", half, oc)])
                    t.matmul(bank(4), ones_b[:], rb2[:], start=(oc == 0),
                             stop=(oc == HC - 1))
                    inst = t.matmul(bank(6), ones_b[:], r2b2[:], start=(oc == 0),
                                    stop=(oc == HC - 1))
                    pe.inc(inst)
                M[("st2", half, oc)] = pe.n
            mk()

        def mkln2(half=half, hof=hof):
            @blk.vector
            def _(v):
                v.wait_ge(pe.h, M[("st2", half, HC - 1)])
                v.tensor_scalar(tmp1k[:, 0:512], bank(4), INV_D, None,
                                op0=ALU.mult)
                v.tensor_scalar(aln[:], bank(6), INV_D, None, op0=ALU.mult)
                v.tensor_tensor(cln[:], tmp1k[:, 0:512], tmp1k[:, 0:512],
                                op=ALU.mult)
                v.tensor_tensor(aln[:], aln[:], cln[:], op=ALU.subtract)
                v.tensor_scalar(aln[:], aln[:], LN_EPS, None, op0=ALU.add)
                v.reciprocal(cln[:], aln[:])
                dv.inc(v.drain())
            M[("ln2a", half)] = dv.n

            @blk.scalar
            def _(s):
                s.wait_ge(dv.h, M[("ln2a", half)])
                s.activation(aln[:], cln[:], AF.Sqrt)
                ac.inc(s.drain())
            M[("ln2b", half)] = ac.n

            @blk.vector
            def _(v):
                v.wait_ge(ac.h, M[("ln2b", half)])
                v.tensor_tensor(cln[:], tmp1k[:, 0:512], aln[:], op=ALU.mult)
                dv.inc(v.drain())
            M[("ln2c", half)] = dv.n

            for oc in range(HC):
                def mk2(oc=oc):
                    @blk.vector
                    def _(v):
                        if oc == 0:
                            v.wait_ge(dv.h, M[("ln2c", half)])
                        if oc >= 1:
                            v.wait_ge(dS.h, M[("ost", half, oc - 1)])
                        sl = slice(512 * oc, 512 * (oc + 1))
                        v.tensor_tensor(tmp1k[:, 512:1024], r2_s[:, sl], aln[:],
                                        op=ALU.mult)
                        v.tensor_tensor(obuf[:], tmp1k[:, 512:1024],
                                        cln[:], op=ALU.subtract)
                        dv.inc(v.drain())
                    M[("oap", half, oc)] = dv.n

                    @blk.sync
                    def _(s):
                        s.wait_ge(dv.h, M[("oap", half, oc)])
                        dS.dma(s.dma_start(
                            out=out[128 * oc:128 * (oc + 1), hof:hof + 512],
                            in_=obuf[:]))
                    M[("ost", half, oc)] = dS.n
                mk2()
        mkln2()

    @blk.sync
    def _(s):
        s.wait_ge(dS.h, dS.n)

    ctx.close()
    return nc


def _prep(inputs):
    x = np.asarray(inputs["L0_attn_hidden_states"], dtype=np.float32)
    w_qkv = np.asarray(inputs["L0_attn_w_qkv"], dtype=np.float32)
    b_qkv = np.asarray(inputs["L0_attn_b_qkv"], dtype=np.float32)
    w_out = np.asarray(inputs["L0_attn_w_out"], dtype=np.float32)
    b_out = np.asarray(inputs["L0_attn_b_out"], dtype=np.float32)
    w_g = np.asarray(inputs["L0_ffn_w_gate"], dtype=np.float32)
    w_u = np.asarray(inputs["L0_ffn_w_up"], dtype=np.float32)
    w_d = np.asarray(inputs["L0_ffn_w_down"], dtype=np.float32)

    xTf = np.ascontiguousarray(x[0].T)
    wqkvT = np.ascontiguousarray(w_qkv.T)
    woutT = np.ascontiguousarray(w_out.T)
    wgT = np.ascontiguousarray(w_g.T)
    wuT = np.ascontiguousarray(w_u.T)
    wdT = np.ascontiguousarray(w_d.T)
    bq = np.ascontiguousarray(b_qkv[0:D].reshape(HC, 128).T)
    bk = np.ascontiguousarray(b_qkv[D:2 * D].reshape(HC, 128).T)
    bo = np.ascontiguousarray(b_out.reshape(HC, 128).T)
    bvb = np.ascontiguousarray(np.broadcast_to(b_qkv[2 * D:3 * D], (128, D)))
    ident = np.eye(128, dtype=np.float32)

    maps = []
    for r in range(N_CORES):
        maps.append({
            "xT": np.ascontiguousarray(xTf[:, T * r:T * (r + 1)]),
            "wqkvT": wqkvT, "b_q": bq, "b_k": bk, "b_v_bc": bvb,
            "woutT": woutT, "b_o": bo,
            "wgT_sh": np.ascontiguousarray(wgT[256 * r:256 * (r + 1), :]),
            "wuT_sh": np.ascontiguousarray(wuT[256 * r:256 * (r + 1), :]),
            "wdT_sh": np.ascontiguousarray(wdT[1024 * r:1024 * (r + 1), :]),
            "identI": ident,
        })
    return maps


def kernel(**inputs):
    maps = _prep(inputs)
    nc = build(debug=tuple(DEBUG))
    res = run_bass_kernel_spmd(nc, maps, core_ids=list(range(N_CORES)))
    outs = [np.asarray(r["out"]) for r in res.results]
    full = np.concatenate([o.T for o in outs], axis=0)
    if DEBUG:
        kernel.last_results = res
    return np.ascontiguousarray(full.reshape(1, S_TOT, D), dtype=np.float32)


# version 13 (final)
# speedup vs baseline: 1.0654x; 1.0654x over previous
import contextlib

import numpy as np

import concourse.bass as bass
import concourse.mybir as mybir
from concourse.bass_utils import run_bass_kernel_spmd

F32 = mybir.dt.float32
BF16 = mybir.dt.bfloat16
I8 = mybir.dt.int8
U8 = mybir.dt.uint8
AF = mybir.ActivationFunctionType
ALU = mybir.AluOpType
AX = mybir.AxisListType

N_CORES = 8
S_TOT = 8192
T = 1024
D = 2048
DFF = 8192
HC = 16
NQB = 8
NKB = 16
NI = NQB * NKB
C_SCALE = float(1.0 / np.sqrt(np.float32(2048.0), dtype=np.float32))
LN_EPS = 1e-5
Q_EPS = 1e-5
INV127 = float(np.float32(1.0) / np.float32(127.0))
INV_D = float(np.float32(1.0) / np.float32(2048.0))
WN_INV = float(np.float32(1.0) / np.float32(8192 * 2048))
CLIP = 1.4999999
TH = 512
KB = 1024

DEBUG = []


class Ctr:
    def __init__(self, handle):
        self.h = handle
        self.n = 0

    def inc(self, inst, k=1):
        inst.then_inc(self.h, k)
        self.n += k
        return self.n

    def dma(self, inst):
        return self.inc(inst, 16)


def build(debug=()):
    nc = bass.Bass("TRN2", target_bir_lowering=False, debug=False)
    ctx = contextlib.ExitStack()

    xT = nc.dram_tensor("xT", [D, T], F32, kind="ExternalInput").ap()
    wqkvT = nc.dram_tensor("wqkvT", [D, 3 * D], F32, kind="ExternalInput").ap()
    b_q = nc.dram_tensor("b_q", [128, HC], F32, kind="ExternalInput").ap()
    b_k = nc.dram_tensor("b_k", [128, HC], F32, kind="ExternalInput").ap()
    b_v_bc = nc.dram_tensor("b_v_bc", [128, D], F32, kind="ExternalInput").ap()
    woutT = nc.dram_tensor("woutT", [D, D], F32, kind="ExternalInput").ap()
    b_o = nc.dram_tensor("b_o", [128, HC], F32, kind="ExternalInput").ap()
    wgT_sh = nc.dram_tensor("wgT_sh", [256, DFF], F32, kind="ExternalInput").ap()
    wuT_sh = nc.dram_tensor("wuT_sh", [256, DFF], F32, kind="ExternalInput").ap()
    wdT_sh = nc.dram_tensor("wdT_sh", [1024, D], F32, kind="ExternalInput").ap()
    identI = nc.dram_tensor("identI", [128, 128], F32, kind="ExternalInput").ap()
    out = nc.dram_tensor("out", [D, T], F32, kind="ExternalOutput").ap()

    kvKV_in = nc.dram_tensor("kvKV_in", [2 * D, T], BF16).ap()
    kvKV_out = nc.dram_tensor("kvKV_out", [N_CORES * 2 * D, T], BF16).ap()
    ar_in = nc.dram_tensor("ar_in", [1, 8], F32).ap()
    ar_out = nc.dram_tensor("ar_out", [1, 8], F32).ap()
    tern_in = nc.dram_tensor("tern_in", [1536, 4096], I8).ap()
    tern_out = nc.dram_tensor("tern_out", [N_CORES * 1536, 4096], I8).ap()
    x1_sc = nc.dram_tensor("x1_sc", [D, T], F32).ap()

    dbg = {}
    for name, shape, dt in (
        ("qt", [128, HC * T], BF16), ("kv", [N_CORES * 2 * D, T], BF16),
        ("s", [128, S_TOT], F32), ("ctxT", [128, HC * T], BF16),
        ("x1", [128, HC * T], F32), ("xq", [128, HC * T], BF16),
        ("tern", [N_CORES * 1536, 4096], I8), ("h", [128, 64 * TH], BF16),
        ("m0", [128, 16], F32), ("s0", [128, S_TOT], F32),
    ):
        if name in debug:
            dbg[name] = nc.dram_tensor("dbg_" + name, shape, dt,
                                       kind="ExternalOutput").ap()

    ARENA_B = 204 * KB
    arena = ctx.enter_context(nc.sbuf_tensor("arena", [128, ARENA_B], U8))
    arena_base = nc.lookup_mloc(arena).addr

    def at(name, shape, dtype, off):
        per_part = int(np.prod(shape[1:])) * mybir.dt.size(dtype)
        assert off % 64 == 0 and off + per_part <= ARENA_B, (name, off, per_part)
        h = nc.virtual_sbuf_tensor(name, shape, dtype)
        nc._tensor(h.name, h.shape, h.dtype, type="SB")
        mloc = nc.lookup_mloc(h)
        mloc.allocated = True
        mloc.addr = arena_base + off
        return h

    ones_f = at("ones_f", [128, 128], F32, 0)
    ones_b = at("ones_b", [128, 128], BF16, 512)
    ident = at("ident", [128, 128], F32, 768)
    bq_s = at("bq_s", [128, HC], F32, 1280)
    bk_s = at("bk_s", [128, HC], F32, 1344)
    bo_s = at("bo_s", [128, HC], F32, 1408)
    svec = at("svec", [128, 8], F32, 1472)
    ivec = at("ivec", [128, 8], F32, 1536)
    arst = at("arst", [128, 16], F32, 1600)
    parts = at("parts", [128, 32], F32, 1664)
    mneg = at("mneg", [128, 1], F32, 1856)
    linv = at("linv", [128, 1], F32, 1920)
    stok = at("stok", [128, 16], F32, 1984)
    itok = at("itok", [128, 16], F32, 2048)
    btile = at("btile", [128, 128], F32, 2112)
    mb_t = at("mb_t", [128, 16], F32, 2624)
    tmp1k = at("tmp1k", [128, 1024], F32, 2688)
    i8tmp = at("i8tmp", [128, 1024], I8, 6784)
    kstage = at("kstage", [128, 2048], BF16, 8192)
    p_bf = at("p_bf", [128, 1024], BF16, 8192)
    pT_b = at("pT_b", [128, 1024], BF16, 10240)
    rbf_t = at("rbf_t", [128, 2 * 1024], BF16, 8192)
    r2bf_t = at("r2bf_t", [128, 2 * 1024], BF16, 12288)
    sgt = at("sgt", [128, 512], F32, 8192)
    upt = at("upt", [128, 512], F32, 10240)
    vstage = at("vstage", [128, 1024], BF16, 12288)
    ctxbf = at("ctxbf", [128, 2048], BF16, 12288)
    x1re = at("x1re", [128, 2 * 512], F32, 12288)
    aln = at("aln", [128, 512], F32, 24576)
    wqb = at("wqb", [128, 2 * D], BF16, 16384)
    wvb = at("wvb", [128, 1024], BF16, 16384)
    wob = at("wob", [128, 2 * D], BF16, 16384)
    wdb = at("wdb", [128, 2048], BF16, 16384)
    cln = at("cln", [128, 512], F32, 20480)
    rb2 = at("rb2", [128, 2 * 512], BF16, 22528)
    r2b2 = at("r2b2", [128, 2 * 512], BF16, 24576)
    hacc = at("hacc", [128, 512], F32, 24576)
    hacc2 = at("hacc2", [128, 512], F32, 28672)
    obuf = at("obuf", [128, 512], F32, 26624)
    B0 = 30 * KB
    wqf32 = at("wqf32", [128, 2 * 2048], F32, B0 + 32 * KB)
    vf32 = at("vf32", [128, 8 * 512], F32, B0 + 48 * KB)
    xt_s = at("xt_s", [128, HC * 1024], F32, B0)
    kbuf = at("kbuf", [128, 2 * HC * 512], BF16, B0)
    vbuf = at("vbuf", [128, 2 * 4 * D], BF16, B0 + 32 * KB)
    h_s = at("h_s", [128, 64 * TH], BF16, B0)
    C0 = 94 * KB
    ternb = at("ternb", [128, 2 * 2048], F32, C0)
    terni = at("terni", [128, 2 * 2048], I8, C0 + 16 * KB)
    bvbc = at("bvbc", [128, D], F32, C0 + 20 * KB)
    S_s = at("S_s", [128, S_TOT], F32, C0)
    xq_s = at("xq_s", [128, HC * 1024], BF16, C0)
    xbf_s = at("xbf_s", [128, HC * 1024], BF16, C0 + 32 * KB)
    ctxT_s = at("ctxT_s", [128, HC * 1024], BF16, C0 + 32 * KB)
    wgub = at("wgub", [128, 2 * 2 * 4096], BF16, C0 + 32 * KB)
    D0 = 158 * KB
    qT_s = at("qT_s", [128, HC * 1024], BF16, D0)
    sxbc = at("sxbc", [128, 1024], F32, D0)
    ixbc = at("ixbc", [128, 1024], F32, D0 + 4 * KB)
    a1_t = at("a1_t", [128, 1024], F32, D0 + 8 * KB)
    c1_t = at("c1_t", [128, 1024], F32, D0 + 12 * KB)
    acc_t = at("acc_t", [128, 1024], F32, D0 + 16 * KB)
    acc2_t = at("acc2_t", [128, 1024], F32, D0 + 20 * KB)
    wof32 = at("wof32", [128, 2048], F32, D0 + 24 * KB)
    r2_s = at("r2_s", [128, HC * TH], F32, D0)
    bg_t = at("bg_t", [128, 1024], F32, D0 + 32 * KB)
    bu_t = at("bu_t", [128, 1024], F32, D0 + 36 * KB)
    shb = at("shb", [128, 512], F32, D0 + 40 * KB)
    ihb = at("ihb", [128, 512], F32, D0 + 42 * KB)
    dsc = at("dsc", [128, 512], F32, D0 + 44 * KB)

    PP = ctx.enter_context(nc.psum_tensor("PP", [128, 4096], F32))

    def bank(i, w=512):
        return PP[:, 512 * i:512 * i + w]

    sem = {}
    for s in ("dS", "dQ", "dG", "dW", "pe", "p2", "dv", "ac", "cc"):
        sem[s] = Ctr(ctx.enter_context(nc.semaphore("sem_" + s)))
    dS, dQ, dG, dW = sem["dS"], sem["dQ"], sem["dG"], sem["dW"]
    pe, p2, dv, ac, cc = sem["pe"], sem["p2"], sem["dv"], sem["ac"], sem["cc"]

    blk = ctx.enter_context(nc.Block())
    M = {}

    @blk.gpsimd
    def _(g):
        dG.inc(g.memset(ones_f[:], 1.0))
        dG.inc(g.memset(ones_b[:], 1.0))
        dG.inc(g.memset(arst[:], 0.0))
        dG.inc(g.memset(parts[:], 0.0))

    @blk.sync
    def _(s):
        dS.dma(s.dma_start(out=xt_s[:].rearrange("p (c t) -> p c t", t=1024),
                           in_=xT.rearrange("(c p) t -> p c t", p=128)))
        dS.dma(s.dma_start(out=ident[:], in_=identI[:, :]))
        dS.dma(s.dma_start(out=bq_s[:], in_=b_q[:, :]))
        dS.dma(s.dma_start(out=bk_s[:], in_=b_k[:, :]))
        dS.dma(s.dma_start(out=bo_s[:], in_=b_o[:, :]))
        dS.dma(s.dma_start(out=bvbc[:], in_=b_v_bc[:, :]))
    M["x_ld"] = dS.n

    def tern_ld_aps(i):
        m, sl = divmod(i, 8)
        buf = ternb[:, 2048 * (i % 2):2048 * (i % 2) + 2048]
        if m < 2:
            wsh = wgT_sh if m == 0 else wuT_sh
            src = wsh[:, 1024 * sl:1024 * (sl + 1)].rearrange(
                "(n p) w -> p n w", p=128)
            dst = buf.rearrange("p (n w) -> p n w", w=1024)
        else:
            g2, c = divmod(sl, 2)
            src = wdT_sh[256 * g2:256 * (g2 + 1),
                         1024 * c:1024 * (c + 1)].rearrange(
                "(p par) o -> p par o", par=2)
            dst = buf.rearrange("p (par o) -> p par o", o=1024)
        return src, dst

    def tern_st_aps(i):
        m, sl = divmod(i, 8)
        buf = terni[:, 2048 * (i % 2):2048 * (i % 2) + 2048]
        if m < 2:
            base = 512 * m
            dst = tern_in[base + sl // 4:base + 512:2,
                          1024 * (sl % 4):1024 * (sl % 4) + 1024].rearrange(
                "(n p) w -> p n w", p=128)
            src = buf.rearrange("p (n w) -> p n w", w=1024)
        else:
            g2, c = divmod(sl, 2)
            dst = tern_in[1024 + 128 * g2:1024 + 128 * (g2 + 1), :].rearrange(
                "r (par z) -> r par z", z=2048)[:, :, 1024 * c:1024 * (c + 1)]
            src = buf.rearrange("p (par o) -> p par o", o=1024)
        return dst, src

    TRED_AT = {8 + 2 * _k: _k for _k in range(24)}
    _dv = 36
    for _i in range(64):
        if _i + 1 < 64:
            _dv += 1
        if _i in TRED_AT:
            _dv += 1
            M[("tred", TRED_AT[_i])] = _dv
        if _i % 16 == 15:
            _dv += 8
    for i in range(24):
        def mk(i=i):
            @blk.scalar
            def _(s):
                if i >= 2:
                    s.wait_ge(dv.h, M[("tred", i - 2)])
                src, dst = tern_ld_aps(i)
                dQ.dma(s.dma_start(out=dst, in_=src))
            M[("tld", i)] = dQ.n
        mk()

    @blk.vector
    def _(v):
        v.wait_ge(dS.h, M["x_ld"])
        for c in (8, 9, 10, 11):
            v.tensor_copy(xbf_s[:, 1024 * c:1024 * (c + 1)],
                          xt_s[:, 1024 * c:1024 * (c + 1)])
        dv.inc(v.drain())
    M["xbf_a"] = dv.n

    @blk.vector
    def _(v):
        for c in (12, 13, 14, 15):
            v.tensor_copy(xbf_s[:, 1024 * c:1024 * (c + 1)],
                          xt_s[:, 1024 * c:1024 * (c + 1)])
        dv.inc(v.drain())
    M["xbf_b"] = dv.n

    @blk.vector
    def _(v):
        for c in range(8):
            v.tensor_copy(xbf_s[:, 1024 * c:1024 * (c + 1)],
                          xt_s[:, 1024 * c:1024 * (c + 1)])
        dv.inc(v.drain())
    M["xbf"] = dv.n

    def w_slab_load(s, oc, fslot):
        srcw = wqkvT[:, 128 * oc:128 * (oc + 1)].rearrange(
            "(c p) o -> p c o", p=128)
        dS.dma(s.dma_start(
            out=wqf32[:, 2048 * fslot:2048 * (fslot + 1)].rearrange(
                "p (c o) -> p c o", o=128), in_=srcw))

    def v_tile_load(s, i):
        ov, c = divmod(i, 16)
        srcw = wqkvT[128 * c:128 * (c + 1),
                     2 * D + 512 * ov:2 * D + 512 * (ov + 1)]
        dS.dma(s.dma_start(out=vf32[:, 512 * (i % 8):512 * (i % 8) + 512],
                           in_=srcw))

    def w_slab_cast(v, slot, fslot):
        v.tensor_copy(wqb[:, D * slot:D * slot + D],
                      wqf32[:, 2048 * fslot:2048 * (fslot + 1)])
        dv.inc(v.drain())

    for j in range(2):
        def mk(j=j):
            @blk.sync
            def _(s):
                s.wait_ge(dv.h, M["xbf_a"])
                w_slab_load(s, 16 + j, j % 2)
            M[("wkl", j)] = dS.n

            if j == 0:
                @blk.vector
                def _(v):
                    v.wait_ge(dS.h, M[("wkl", j)])
                    w_slab_cast(v, j % 2, j % 2)
                M[("wk", j)] = dv.n
        mk()

    for j in range(16):
        def mk(j=j):
            @blk.tensor
            def _(t):
                t.wait_ge(dv.h, M[("wk", j)])
                if j == 0:
                    t.wait_ge(dv.h, M["xbf"])
                if j >= 2:
                    t.wait_ge(dv.h, M[("kev", j - 2)])
                for half in range(2):
                    b = 2 * (j % 2) + half
                    for c in range(HC):
                        inst = t.matmul(
                            bank(b),
                            wqb[:, D * (j % 2) + 128 * c:D * (j % 2) + 128 * (c + 1)],
                            xbf_s[:, 1024 * c + 512 * half:1024 * c + 512 * (half + 1)],
                            start=(c == 0), stop=(c == HC - 1))
                pe.inc(t.drain())
            M[("kmm", j)] = pe.n

            @blk.vector
            def _(v):
                v.wait_ge(pe.h, M[("kmm", j)])
                if j >= 2:
                    v.wait_ge(dS.h, M[("kst", j - 2)])
                for half in range(2):
                    inst = v.tensor_scalar(
                        kstage[:, 1024 * (j % 2) + 512 * half:
                               1024 * (j % 2) + 512 * (half + 1)],
                        bank(2 * (j % 2) + half), bk_s[:, j:j + 1], None,
                        op0=ALU.add)
                dv.inc(v.drain())
            M[("kev", j)] = dv.n

            @blk.sync
            def _(s):
                s.wait_ge(dv.h, M[("kev", j)])
                dS.dma(s.dma_start(
                    out=kvKV_in[128 * j:128 * (j + 1), :],
                    in_=kstage[:, 1024 * (j % 2):1024 * (j % 2) + 1024]))
            M[("kst", j)] = dS.n

            if j + 2 < 16:
                @blk.sync
                def _(s):
                    s.wait_ge(dv.h, M[("wk", j)])
                    w_slab_load(s, 16 + j + 2, j % 2)
                M[("wkl", j + 2)] = dS.n
            if j >= 8:
                @blk.sync
                def _(s):
                    if j == 8:
                        s.wait_ge(dv.h, M["xbf_b"])
                    v_tile_load(s, j - 8)
                M[("wvl", j - 8)] = dS.n
            if j + 1 < 16:
                @blk.vector
                def _(v):
                    v.wait_ge(dS.h, M[("wkl", j + 1)])
                    if j + 1 >= 2:
                        v.wait_ge(pe.h, M[("kmm", j - 1)])
                    w_slab_cast(v, (j + 1) % 2, (j + 1) % 2)
                M[("wk", j + 1)] = dv.n
        mk()

    for i in range(1):
        def mk(i=i):
            @blk.vector
            def _(v):
                v.wait_ge(dS.h, M[("wvl", i)])
                v.wait_ge(pe.h, M[("kmm", 15)])
                v.tensor_copy(wvb[:, 512 * (i % 2):512 * (i % 2) + 512],
                              vf32[:, 512 * (i % 8):512 * (i % 8) + 512])
                dv.inc(v.drain())
            M[("wv", i)] = dv.n
        mk()

    for ov in range(4):
        for c in range(16):
            i = 16 * ov + c

            def mk(i=i, ov=ov, c=c):
                @blk.tensor
                def _(t):
                    t.wait_ge(dv.h, M[("wv", i)])
                    if c == 0:
                        t.wait_ge(dv.h, M[("kev", 15)] if ov == 0
                                  else M[("vev", ov - 1, 7)])
                    for tc in range(8):
                        inst = t.matmul(
                            bank(tc),
                            xbf_s[:, 1024 * c + 128 * tc:1024 * c + 128 * (tc + 1)],
                            wvb[:, 512 * (i % 2):512 * (i % 2) + 512],
                            start=(c == 0), stop=(c == 15))
                    pe.inc(t.drain())
                M[("vmm", i)] = pe.n

                if i + 8 < 64:
                    @blk.sync
                    def _(s):
                        s.wait_ge(dv.h, M[("wv", i)])
                        v_tile_load(s, i + 8)
                    M[("wvl", i + 8)] = dS.n
                if i in (16, 17):
                    @blk.sync
                    def _(s):
                        s.wait_ge(dv.h, M[("wk", 14 + (i - 16))])
                        w_slab_load(s, i - 16, (i - 16) % 2)
                    M[("wql", i - 16)] = dS.n
                if i + 1 < 64:
                    @blk.vector
                    def _(v):
                        v.wait_ge(dS.h, M[("wvl", i + 1)])
                        if i + 1 >= 2:
                            v.wait_ge(pe.h, M[("vmm", i - 1)])
                        v.tensor_copy(
                            wvb[:, 512 * ((i + 1) % 2):512 * ((i + 1) % 2) + 512],
                            vf32[:, 512 * ((i + 1) % 8):
                                 512 * ((i + 1) % 8) + 512])
                        dv.inc(v.drain())
                    M[("wv", i + 1)] = dv.n
                if i in TRED_AT:
                    k = TRED_AT[i]

                    @blk.vector
                    def _(v):
                        v.wait_ge(dQ.h, M[("tld", k)])
                        v.tensor_reduce(
                            parts[:, k:k + 1],
                            ternb[:, 2048 * (k % 2):2048 * (k % 2) + 2048],
                            axis=AX.X, op=ALU.add, apply_absolute_value=True)
                        dv.inc(v.drain())
                    assert M[("tred", k)] == dv.n, (k, M[("tred", k)], dv.n)
            mk()

        for tc in range(8):
            def mk(ov=ov, tc=tc):
                j = 8 * ov + tc

                @blk.vector
                def _(v):
                    v.wait_ge(pe.h, M[("vmm", 16 * ov + 15)])
                    if j >= 2:
                        v.wait_ge(dS.h, M[("vst", j - 2)])
                    v.tensor_tensor(
                        vstage[:, 512 * (j % 2):512 * (j % 2) + 512], bank(tc),
                        bvbc[:, 512 * ov:512 * (ov + 1)], op=ALU.add)
                    dv.inc(v.drain())
                M[("vev", ov, tc)] = dv.n

                @blk.sync
                def _(s):
                    s.wait_ge(dv.h, M[("vev", ov, tc)])
                    kvv = kvKV_in[D:2 * D, :].rearrange("(t two) c -> t (two c)", two=2)
                    dS.dma(s.dma_start(
                        out=kvv[128 * tc:128 * (tc + 1), 512 * ov:512 * (ov + 1)],
                        in_=vstage[:, 512 * (j % 2):512 * (j % 2) + 512]))
                M[("vst", j)] = dS.n
            mk()

    @blk.tensor
    def _(t):
        t.wait_ge(dv.h, M[("tred", 23)])
        t.wait_ge(dv.h, M[("vev", 3, 7)])
        pe.inc(t.matmul(PP[0:1, 3584:3608], ones_f[:, 0:1], parts[:, 0:24],
                        start=True, stop=True))
    M["psum_mm"] = pe.n

    @blk.vector
    def _(v):
        v.wait_ge(pe.h, M["psum_mm"])
        v.tensor_reduce(arst[0:1, 0:3],
                        PP[0:1, 3584:3608].rearrange("a (m s) -> a m s", s=8),
                        axis=AX.X, op=ALU.add)
        dv.inc(v.drain())
    M["ar_ready"] = dv.n

    @blk.scalar
    def _(s):
        s.wait_ge(dv.h, M["ar_ready"])
        dQ.dma(s.dma_start(out=ar_in[:, :], in_=arst[0:1, 0:8]))
    M["ar_st"] = dQ.n

    @blk.gpsimd
    def _(g):
        g.wait_ge(dQ.h, M["ar_st"])
        cc.inc(g.collective_compute(
            "AllReduce", ALU.add, replica_groups=[list(range(N_CORES))],
            ins=[ar_in[:, :]], outs=[ar_out[:, :]]))
    M["ar_cc"] = cc.n

    for j in range(1):
        def mk(j=j):
            @blk.vector
            def _(v):
                v.wait_ge(dS.h, M[("wql", j)])
                v.wait_ge(pe.h, M[("vmm", 63)])
                w_slab_cast(v, j % 2, j % 2)
            M[("wq", j)] = dv.n
        mk()

    for j in range(16):
        def mk(j=j):
            @blk.tensor
            def _(t):
                t.wait_ge(dv.h, M[("wq", j)])
                if j < 2:
                    t.wait_ge(dv.h, M[("vev", 3, 7)])
                if j >= 2:
                    t.wait_ge(dv.h, M[("qev", j - 2)])
                for half in range(2):
                    b = 2 * (j % 2) + half
                    for c in range(HC):
                        inst = t.matmul(
                            bank(b),
                            wqb[:, D * (j % 2) + 128 * c:D * (j % 2) + 128 * (c + 1)],
                            xbf_s[:, 1024 * c + 512 * half:1024 * c + 512 * (half + 1)],
                            start=(c == 0), stop=(c == HC - 1))
                pe.inc(t.drain())
            M[("qmm", j)] = pe.n

            @blk.vector
            def _(v):
                v.wait_ge(pe.h, M[("qmm", j)])
                for half in range(2):
                    inst = v.tensor_scalar(
                        qT_s[:, 1024 * j + 512 * half:1024 * j + 512 * (half + 1)],
                        bank(2 * (j % 2) + half), bq_s[:, j:j + 1], None,
                        op0=ALU.add)
                dv.inc(v.drain())
            M[("qev", j)] = dv.n

            if j + 2 < 16:
                @blk.sync
                def _(s):
                    s.wait_ge(dv.h, M[("wq", j)])
                    w_slab_load(s, j + 2, j % 2)
                M[("wql", j + 2)] = dS.n
            if j + 1 < 16:
                @blk.vector
                def _(v):
                    v.wait_ge(dS.h, M[("wql", j + 1)])
                    if j + 1 >= 2:
                        v.wait_ge(pe.h, M[("qmm", j - 1)])
                    w_slab_cast(v, (j + 1) % 2, (j + 1) % 2)
                M[("wq", j + 1)] = dv.n
        mk()

    @blk.scalar
    def _(s):
        s.wait_ge(cc.h, M["ar_cc"])
        dQ.dma(s.dma_start(out=arst[0:1, 8:16], in_=ar_out[:, :]))
    M["ar_ld"] = dQ.n

    @blk.tensor
    def _(t):
        t.wait_ge(dQ.h, M["ar_ld"])
        pe.inc(t.matmul(PP[:, 3584:3592], ones_f[0:1, 0:128], arst[0:1, 8:16],
                        start=True, stop=True))
    M["ar_bc"] = pe.n

    @blk.vector
    def _(v):
        v.wait_ge(pe.h, M["ar_bc"])
        v.tensor_scalar(svec[:], PP[:, 3584:3592], WN_INV, LN_EPS,
                        op0=ALU.mult, op1=ALU.add)
        v.drain()
        v.reciprocal(ivec[:], svec[:])
        dv.inc(v.drain())
    M["ivec"] = dv.n

    for i in range(24):
        def mk(i=i):
            @blk.scalar
            def _(s):
                if i == 0:
                    s.wait_ge(dv.h, M[("tred", 23)])
                if i >= 2:
                    s.wait_ge(dv.h, M[("tq", i - 2)])
                src, dst = tern_ld_aps(i)
                dQ.dma(s.dma_start(out=dst, in_=src))
            M[("tld2", i)] = dQ.n

            @blk.vector
            def _(v):
                v.wait_ge(dQ.h, M[("tld2", i)])
                if i == 0:
                    v.wait_ge(dv.h, M["ivec"])
                if i >= 2:
                    v.wait_ge(dQ.h, M[("tst", i - 2)])
                m = i // 8
                buf = ternb[:, 2048 * (i % 2):2048 * (i % 2) + 2048]
                v.tensor_scalar(buf, buf, ivec[:, m:m + 1], -CLIP,
                                op0=ALU.mult, op1=ALU.max)
                v.tensor_scalar(
                    terni[:, 2048 * (i % 2):2048 * (i % 2) + 2048], buf,
                    CLIP, None, op0=ALU.min)
                dv.inc(v.drain())
            M[("tq", i)] = dv.n

            @blk.scalar
            def _(s):
                s.wait_ge(dv.h, M[("tq", i)])
                dst, src = tern_st_aps(i)
                dQ.dma(s.dma_start(out=dst, in_=src))
            M[("tst", i)] = dQ.n
        mk()


    @blk.gpsimd
    def _(g):
        g.wait_ge(dS.h, M[("vst", 31)])
        g.wait_ge(cc.h, M["ar_cc"])
        g.wait_ge(dv.h, M["ivec"])
        cc.inc(g.collective_compute(
            "AllGather", ALU.bypass, replica_groups=[list(range(N_CORES))],
            ins=[kvKV_in[:, :]], outs=[kvKV_out[:, :]]))
    M["ag_k"] = cc.n
    M["ag_v"] = cc.n

    for i in range(2):
        def mk(i=i):
            @blk.gpsimd
            def _(g):
                g.wait_ge(cc.h, M["ag_v"])
                rp, sub = divmod(i % 16, 2)
                src = kvKV_out[2 * D * rp + D:2 * D * (rp + 1), :].rearrange(
                    "(t two) c -> t (two c)", two=2)[512 * sub:512 * (sub + 1), :]
                dW.dma(g.dma_start(
                    out=vbuf[:, 8192 * (i % 2):8192 * (i % 2) + 8192].rearrange(
                        "p (n c) -> p n c", c=D),
                    in_=src.rearrange("(n p) c -> p n c", p=128)))
            M[("vld", i)] = dW.n
        mk()

    if "qt" in dbg:
        @blk.sync
        def _(s):
            s.wait_ge(dv.h, M[("qev", 15)])
            dS.dma(s.dma_start(out=dbg["qt"][:, :], in_=qT_s[:]))
    if "kv" in dbg:
        @blk.sync
        def _(s):
            s.wait_ge(cc.h, M["ag_v"])
            dS.dma(s.dma_start(out=dbg["kv"][:, :], in_=kvKV_out[:, :]))

    for i in range(2):
        def mk(i=i):
            @blk.scalar
            def _(s):
                s.wait_ge(cc.h, M["ag_v"])
                if i < 2:
                    s.wait_ge(dv.h, M["xbf"])
                rp, sub = divmod(i % 16, 2)
                src = kvKV_out[2 * D * rp:2 * D * rp + D,
                              512 * sub:512 * (sub + 1)].rearrange(
                    "(c p) w -> p c w", p=128)
                dQ.dma(s.dma_start(
                    out=kbuf[:, 8192 * (i % 2):8192 * (i % 2) + 8192].rearrange(
                        "p (c w) -> p c w", w=512), in_=src))
            M[("kld", i)] = dQ.n
        mk()

    for qb in range(NQB):
        for kb in range(NKB):
            i = 16 * qb + kb

            def mk(i=i, qb=qb, kb=kb):
                @blk.tensor
                def _(t):
                    t.wait_ge(dQ.h, M[("kld", i)])
                    if i < 2:
                        t.wait_ge(dv.h, M[("qev", 15)])
                    if i >= 2:
                        t.wait_ge(dv.h, M[("ev", i - 2)])
                    for c in range(HC):
                        inst = t.matmul(
                            bank(i % 2),
                            qT_s[:, 1024 * c + 128 * qb:1024 * c + 128 * (qb + 1)],
                            kbuf[:, 8192 * (i % 2) + 512 * c:
                                 8192 * (i % 2) + 512 * (c + 1)],
                            start=(c == 0), stop=(c == HC - 1))
                    pe.inc(inst)
                M[("p1", i)] = pe.n

                @blk.vector
                def _(v):
                    v.wait_ge(pe.h, M[("p1", i)])
                    if qb > 0 and kb == 0:
                        v.wait_ge(ac.h, M[("id", qb - 1)])
                    v.tensor_scalar(S_s[:, 512 * kb:512 * (kb + 1)],
                                    bank(i % 2), C_SCALE, None,
                                    op0=ALU.mult)
                    inst = v.tensor_reduce(mb_t[:, kb:kb + 1], bank(i % 2),
                                           axis=AX.X, op=ALU.max)
                    dv.inc(inst)
                M[("ev", i)] = dv.n

                if i + 2 < NI:
                    @blk.scalar
                    def _(s):
                        j = i + 2
                        s.wait_ge(pe.h, M[("p1", i)])
                        rp, sub = divmod(j % 16, 2)
                        src = kvKV_out[2 * D * rp:2 * D * rp + D,
                                      512 * sub:512 * (sub + 1)].rearrange(
                            "(c p) w -> p c w", p=128)
                        dQ.dma(s.dma_start(
                            out=kbuf[:, 8192 * (j % 2):8192 * (j % 2) + 8192]
                            .rearrange("p (c w) -> p c w", w=512), in_=src))
                    M[("kld", i + 2)] = dQ.n
            mk()

        def mkq(qb=qb):
            @blk.vector
            def _(v):
                v.wait_ge(dv.h, M[("ev", 16 * qb + 15)])
                v.drain()
                v.tensor_reduce(mneg[:], mb_t[:], axis=AX.X, op=ALU.max,
                                negate=True)
                v.drain()
                v.tensor_scalar(mneg[:], mneg[:], C_SCALE, None,
                                op0=ALU.mult)
                dv.inc(v.drain())
            M[("mx", qb)] = dv.n
        mkq()

        for kb in range(NKB):
            i = 16 * qb + kb

            def mk(i=i, qb=qb, kb=kb):
                @blk.scalar
                def _(s):
                    s.wait_ge(dv.h, M[("mx", qb)])
                    if i >= 2:
                        s.wait_ge(dS.h, M[("tr", i - 2)])
                    s.activation(p_bf[:, 512 * (i % 2):512 * (i % 2) + 512],
                                 S_s[:, 512 * kb:512 * (kb + 1)],
                                 AF.Exp, bias=mneg[:])
                    ac.inc(s.drain())
                M[("exp", i)] = ac.n

                @blk.sync
                def _(s):
                    s.wait_ge(ac.h, M[("exp", i)])
                    if i >= 2:
                        s.wait_ge(p2.h, M[("p2", i - 2)])
                    for j in range(4):
                        dS.dma(s.dma_start(
                            out=pT_b[:, 512 * (i % 2) + 128 * j:
                                     512 * (i % 2) + 128 * (j + 1)],
                            in_=p_bf[:, 512 * (i % 2) + 128 * j:
                                     512 * (i % 2) + 128 * (j + 1)],
                            transpose=True))
                M[("tr", i)] = dS.n

                @blk.tensor
                def _(t):
                    t.wait_ge(dS.h, M[("tr", i)])
                    t.wait_ge(dW.h, M[("vld", i)])
                    if kb == 0 and qb > 0:
                        t.wait_ge(ac.h, M[("id", qb - 1)])
                    for tkc in range(4):
                        lhs = pT_b[:, 512 * (i % 2) + 128 * tkc:
                                   512 * (i % 2) + 128 * (tkc + 1)]
                        first = (kb == 0 and tkc == 0)
                        last = (kb == 15 and tkc == 3)
                        for hb in range(4):
                            t.matmul(
                                PP[:, 1024 + 512 * hb:1536 + 512 * hb], lhs,
                                vbuf[:, 8192 * (i % 2) + 2048 * tkc + 512 * hb:
                                     8192 * (i % 2) + 2048 * tkc + 512 * (hb + 1)],
                                start=first, stop=last)
                        inst = t.matmul(PP[:, 3072:3073], lhs, ones_b[:, 0:1],
                                        start=first, stop=last)
                    p2.inc(inst)
                M[("p2", i)] = p2.n

                if i + 2 < NI:
                    @blk.gpsimd
                    def _(g):
                        j = i + 2
                        g.wait_ge(p2.h, M[("p2", i)])
                        rp, sub = divmod(j % 16, 2)
                        src = kvKV_out[2 * D * rp + D:2 * D * (rp + 1), :].rearrange(
                            "(t two) c -> t (two c)", two=2)[
                            512 * sub:512 * (sub + 1), :]
                        dW.dma(g.dma_start(
                            out=vbuf[:, 8192 * (j % 2):8192 * (j % 2) + 8192]
                            .rearrange("p (n c) -> p n c", c=D),
                            in_=src.rearrange("(n p) c -> p n c", p=128)))
                    M[("vld", i + 2)] = dW.n
            mk()

        def mkq2(qb=qb):
            @blk.vector
            def _(v):
                v.wait_ge(p2.h, M[("p2", 16 * qb + 15)])
                if qb > 0:
                    v.wait_ge(ac.h, M[("id", qb - 1)])
                v.reciprocal(linv[:], PP[:, 3072:3073])
                dv.inc(v.drain())
            M[("lv", qb)] = dv.n

            if qb == 0 and "m0" in dbg:
                @blk.vector
                def _(v):
                    v.tensor_copy(tmp1k[:, 0:1], mneg[:])
                    v.tensor_copy(tmp1k[:, 1:2], linv[:])
                    v.tensor_copy(tmp1k[:, 2:3], PP[:, 3072:3073])
                    v.drain()
                    v.tensor_copy(tmp1k[:, 3:16], mb_t[:, 0:13])
                    dv.inc(v.drain())
                M["m0d"] = dv.n

                @blk.sync
                def _(s):
                    s.wait_ge(dv.h, M["m0d"])
                    dS.dma(s.dma_start(out=dbg["m0"][:, :], in_=tmp1k[:, 0:16]))
                M["m0st"] = dS.n

            @blk.scalar
            def _(s):
                s.wait_ge(dv.h, M[("lv", qb)])
                if qb > 0:
                    s.wait_ge(dS.h, M[("ctr", qb - 1)])
                s.activation(ctxbf[:], PP[:, 1024:3072], AF.Identity,
                             scale=linv[:])
                ac.inc(s.drain())
            M[("id", qb)] = ac.n

            if qb < 7:
                @blk.sync
                def _(s):
                    s.wait_ge(ac.h, M[("id", qb)])
                    for c in range(HC):
                        dS.dma(s.dma_start(
                            out=ctxT_s[:, 1024 * c + 128 * qb:
                                       1024 * c + 128 * (qb + 1)],
                            in_=ctxbf[:, 128 * c:128 * (c + 1)], transpose=True))
                M[("ctr", qb)] = dS.n
            else:
                @blk.scalar
                def _(s):
                    s.wait_ge(ac.h, M[("id", qb)])
                    for c in range(HC):
                        dQ.dma(s.dma_start(
                            out=ctxT_s[:, 1024 * c + 128 * qb:
                                       1024 * c + 128 * (qb + 1)],
                            in_=ctxbf[:, 128 * c:128 * (c + 1)], transpose=True))
                M[("ctr", qb)] = dQ.n
        mkq2()
        if qb == 0 and "s0" in dbg:
            @blk.sync
            def _(s):
                s.wait_ge(ac.h, M[("exp", 15)])
                dS.dma(s.dma_start(out=dbg["s0"][:, :], in_=S_s[:]))

    if "s" in dbg:
        @blk.sync
        def _(s):
            s.wait_ge(ac.h, M[("id", NQB - 1)])
            dS.dma(s.dma_start(out=dbg["s"][:, :], in_=S_s[:]))
    if "ctxT" in dbg:
        @blk.sync
        def _(s):
            s.wait_ge(dQ.h, M[("ctr", NQB - 1)])
            dS.dma(s.dma_start(out=dbg["ctxT"][:, :], in_=ctxT_s[:]))

    @blk.gpsimd
    def _(g):
        g.wait_ge(dQ.h, M[("tst", 23)])
        g.wait_ge(cc.h, M["ag_v"])
        g.wait_ge(ac.h, M[("id", NQB - 1)])
        cc.inc(g.collective_compute(
            "AllGather", ALU.bypass, replica_groups=[list(range(N_CORES))],
            ins=[tern_in[:, :]], outs=[tern_out[:, :]]))
    M["tern_ag"] = cc.n
    if "tern" in dbg:
        @blk.sync
        def _(s):
            s.wait_ge(cc.h, M["tern_ag"])
            dS.dma(s.dma_start(out=dbg["tern"][:, :], in_=tern_out[:, :]))

    @blk.sync
    def _(s):
        s.wait_ge(pe.h, M[("p1", NI - 1)])
        dS.dma(s.dma_start(
            out=xt_s[:, 0:8192].rearrange("p (c t) -> p c t", t=1024),
            in_=xT[0:1024, :].rearrange("(c p) t -> p c t", p=128)))
    M["x_reA"] = dS.n

    @blk.sync
    def _(s):
        s.wait_ge(p2.h, M[("p2", NI - 1)])
        dS.dma(s.dma_start(
            out=xt_s[:, 8192:16384].rearrange("p (c t) -> p c t", t=1024),
            in_=xT[1024:2048, :].rearrange("(c p) t -> p c t", p=128)))
    M["x_re"] = dS.n

    def wo_slab_load(s, oc):
        srcw = woutT[:, 128 * oc:128 * (oc + 1)].rearrange(
            "(c p) o -> p c o", p=128)
        dS.dma(s.dma_start(
            out=wof32[:].rearrange("p (c o) -> p c o", o=128), in_=srcw))

    for oc in range(1):
        def mk(oc=oc):
            @blk.sync
            def _(s):
                s.wait_ge(pe.h, M[("p1", NI - 1)])
                wo_slab_load(s, oc)
            M[("wol", oc)] = dS.n

            @blk.vector
            def _(v):
                v.wait_ge(dS.h, M[("wol", oc)])
                v.tensor_copy(wob[:, D * (oc % 2):D * (oc % 2) + D], wof32[:])
                dv.inc(v.drain())
            M[("wo", oc)] = dv.n
        mk()

    for oc in range(HC):
        def mk(oc=oc):
            @blk.tensor
            def _(t):
                t.wait_ge(dv.h, M[("wo", oc)])
                if oc == 0:
                    t.wait_ge(dQ.h, M[("ctr", NQB - 1)])
                if oc >= 2:
                    t.wait_ge(dv.h, M[("stt", oc - 2)])
                for half in range(2):
                    b = 2 * (oc % 2) + half
                    for c in range(HC):
                        inst = t.matmul(
                            bank(b),
                            wob[:, D * (oc % 2) + 128 * c:D * (oc % 2) + 128 * (c + 1)],
                            ctxT_s[:, 1024 * c + 512 * half:
                                   1024 * c + 512 * (half + 1)],
                            start=(c == 0), stop=(c == HC - 1))
                pe.inc(inst)
            M[("op", oc)] = pe.n

            if oc + 1 < HC:
                @blk.sync
                def _(s):
                    s.wait_ge(dv.h, M[("wo", oc)])
                    wo_slab_load(s, oc + 1)
                M[("wol", oc + 1)] = dS.n

                @blk.vector
                def _(v):
                    v.wait_ge(dS.h, M[("wol", oc + 1)])
                    if oc + 1 >= 2:
                        v.wait_ge(pe.h, M[("op", oc - 1)])
                    v.tensor_copy(wob[:, D * ((oc + 1) % 2):
                                      D * ((oc + 1) % 2) + D], wof32[:])
                    dv.inc(v.drain())
                M[("wo", oc + 1)] = dv.n

            @blk.vector
            def _(v):
                v.wait_ge(pe.h, M[("op", oc)])
                v.wait_ge(dS.h, M["x_reA"] if oc < 8 else M["x_re"])
                if oc >= 2:
                    v.wait_ge(pe.h, M[("st1", oc - 2)])
                for half in range(2):
                    sl = slice(1024 * oc + 512 * half,
                               1024 * oc + 512 * (half + 1))
                    v.scalar_tensor_tensor(xt_s[:, sl], bank(2 * (oc % 2) + half),
                                           bo_s[:, oc:oc + 1], xt_s[:, sl],
                                           op0=ALU.add, op1=ALU.add)
                v.tensor_copy(rbf_t[:, 1024 * (oc % 2):1024 * (oc % 2) + 1024],
                              xt_s[:, 1024 * oc:1024 * (oc + 1)])
                dv.inc(v.drain())
            M[("stt", oc)] = dv.n

            @blk.scalar
            def _(s):
                s.wait_ge(dv.h, M[("stt", oc)])
                s.activation(r2bf_t[:, 1024 * (oc % 2):1024 * (oc % 2) + 1024],
                             xt_s[:, 1024 * oc:1024 * (oc + 1)], AF.Square)
                ac.inc(s.drain())
            M[("sq", oc)] = ac.n

            @blk.tensor
            def _(t):
                t.wait_ge(ac.h, M[("sq", oc)])
                if oc == 0:
                    t.wait_ge(dv.h, M["ivec"])
                ro = 1024 * (oc % 2)
                for half in range(2):
                    t.matmul(bank(4 + half), ones_b[:],
                             rbf_t[:, ro + 512 * half:ro + 512 * (half + 1)],
                             start=(oc == 0), stop=(oc == HC - 1))
                    inst = t.matmul(bank(6 + half), ones_b[:],
                                    r2bf_t[:, ro + 512 * half:ro + 512 * (half + 1)],
                                    start=(oc == 0), stop=(oc == HC - 1))
                pe.inc(inst)
            M[("st1", oc)] = pe.n
        mk()

    @blk.vector
    def _(v):
        v.wait_ge(pe.h, M[("st1", HC - 1)])
        v.tensor_scalar(tmp1k[:], PP[:, 2048:3072], INV_D, None, op0=ALU.mult)
        v.tensor_scalar(a1_t[:], PP[:, 3072:4096], INV_D, None, op0=ALU.mult)
        v.tensor_tensor(c1_t[:], tmp1k[:], tmp1k[:], op=ALU.mult)
        v.tensor_tensor(a1_t[:], a1_t[:], c1_t[:], op=ALU.subtract)
        v.tensor_scalar(a1_t[:], a1_t[:], LN_EPS, None, op0=ALU.add)
        v.reciprocal(c1_t[:], a1_t[:])
        dv.inc(v.drain())
    M["ln1a"] = dv.n

    @blk.scalar
    def _(s):
        s.wait_ge(dv.h, M["ln1a"])
        s.activation(a1_t[:], c1_t[:], AF.Sqrt)
        ac.inc(s.drain())
    M["ln1b"] = ac.n

    @blk.vector
    def _(v):
        v.wait_ge(ac.h, M["ln1b"])
        v.tensor_tensor(c1_t[:], tmp1k[:], a1_t[:], op=ALU.mult)
        dv.inc(v.drain())
    M["ln1c"] = dv.n

    for oc in range(HC):
        def mk(oc=oc):
            @blk.vector
            def _(v):
                if oc == 0:
                    v.wait_ge(dv.h, M["ln1c"])
                sl = slice(1024 * oc, 1024 * (oc + 1))
                v.tensor_tensor(tmp1k[:], xt_s[:, sl], a1_t[:], op=ALU.mult)
                v.tensor_tensor(xt_s[:, sl], tmp1k[:], c1_t[:], op=ALU.subtract)
                if oc == 0:
                    v.tensor_tensor(acc_t[:], xt_s[:, sl], xt_s[:, sl],
                                    op=ALU.max)
                    inst = v.tensor_tensor(acc2_t[:], xt_s[:, sl], xt_s[:, sl],
                                           op=ALU.min)
                else:
                    v.tensor_tensor(acc_t[:], acc_t[:], xt_s[:, sl],
                                    op=ALU.max)
                    inst = v.tensor_tensor(acc2_t[:], acc2_t[:], xt_s[:, sl],
                                           op=ALU.min)
                dv.inc(v.drain() if oc == HC - 1 else inst)
            M[("ap1", oc)] = dv.n
        mk()

    @blk.sync
    def _(s):
        s.wait_ge(dv.h, M[("ap1", HC - 1)])
        dS.dma(s.dma_start(out=x1_sc.rearrange("(c p) t -> p c t", p=128),
                           in_=xt_s[:].rearrange("p (c t) -> p c t", t=1024)))
    M["x1_sp"] = dS.n
    if "x1" in dbg:
        @blk.sync
        def _(s):
            s.wait_ge(dv.h, M[("ap1", HC - 1)])
            dS.dma(s.dma_start(out=dbg["x1"][:, :], in_=xt_s[:]))

    def quant_scales(tag, nblk, acc_ap, dv_ready, sx_out, ix_out):
        for tb in range(nblk):
            def mk(tb=tb):
                @blk.tensor
                def _(t):
                    if tb == 0:
                        t.wait_ge(dv.h, dv_ready)
                    else:
                        t.wait_ge(dv.h, M[(tag, "red", tb - 1)])
                    pe.inc(t.matmul(PP[:, 3584:3712],
                                    acc_ap[:, 128 * tb:128 * (tb + 1)], ident[:],
                                    is_transpose=True, start=True, stop=True))
                M[(tag, "tp", tb)] = pe.n

                @blk.vector
                def _(v):
                    v.wait_ge(pe.h, M[(tag, "tp", tb)])
                    v.tensor_reduce(stok[:, tb:tb + 1], PP[:, 3584:3712],
                                    axis=AX.X, op=ALU.max)
                    dv.inc(v.drain())
                M[(tag, "red", tb)] = dv.n
            mk()

        @blk.vector
        def _(v):
            v.wait_ge(dv.h, M[(tag, "red", nblk - 1)])
            v.drain()
            v.tensor_scalar(stok[:, 8:8 + nblk], stok[:, 0:nblk], Q_EPS, INV127,
                            op0=ALU.max, op1=ALU.mult)
            v.drain()
            v.reciprocal(itok[:, 0:nblk], stok[:, 8:8 + nblk])
            dv.inc(v.drain())
        M[(tag, "tok")] = dv.n

        prevtp = None
        for tb in range(nblk):
            for which in ("s", "i"):
                tokap = stok[:, 8 + tb:9 + tb] if which == "s" else \
                    itok[:, tb:tb + 1]
                dst = sx_out if which == "s" else ix_out

                def mk(tb=tb, which=which, tokap=tokap, dst=dst, prevtp=prevtp):
                    @blk.vector
                    def _(v):
                        if prevtp is None:
                            v.wait_ge(dv.h, M[(tag, "tok")])
                        else:
                            v.wait_ge(pe.h, prevtp)
                        v.tensor_scalar(btile[:], ones_f[:], tokap, None,
                                        op0=ALU.mult)
                        dv.inc(v.drain())
                    M[(tag, "bt", tb, which)] = dv.n

                    @blk.tensor
                    def _(t):
                        t.wait_ge(dv.h, M[(tag, "bt", tb, which)])
                        pe.inc(t.matmul(PP[:, 3584:3712], btile[:], ident[:],
                                        is_transpose=True, start=True, stop=True))
                    M[(tag, "btp", tb, which)] = pe.n

                    @blk.vector
                    def _(v):
                        v.wait_ge(pe.h, M[(tag, "btp", tb, which)])
                        v.tensor_copy(dst[:, 128 * tb:128 * (tb + 1)],
                                      PP[:, 3584:3712])
                        dv.inc(v.drain())
                    M[(tag, "bc", tb, which)] = dv.n
                mk()
                prevtp = M[(tag, "btp", tb, which)]
        M[(tag, "done")] = dv.n

    @blk.vector
    def _(v):
        v.wait_ge(dv.h, M[("ap1", HC - 1)])
        v.wait_ge(cc.h, M["tern_ag"])
        v.tensor_scalar(acc2_t[:], acc2_t[:], -1.0, None, op0=ALU.mult)
        v.tensor_tensor(acc_t[:], acc_t[:], acc2_t[:], op=ALU.max)
        dv.inc(v.drain())
    M["qx_acc"] = dv.n

    quant_scales("qx", 8, acc_t, M["qx_acc"], sxbc, ixbc)

    @blk.vector
    def _(v):
        v.wait_ge(dv.h, M[("qx", "done")])
        v.wait_ge(dQ.h, M[("tst", 23)])
        v.tensor_scalar(bg_t[:], sxbc[:], svec[:, 0:1], None, op0=ALU.mult)
        v.tensor_scalar(bu_t[:], sxbc[:], svec[:, 1:2], None, op0=ALU.mult)
        for oc in range(HC):
            sl = slice(1024 * oc, 1024 * (oc + 1))
            v.tensor_tensor(i8tmp[:], xt_s[:, sl], ixbc[:], op=ALU.mult)
            v.tensor_copy(xq_s[:, sl], i8tmp[:])
        dv.inc(v.drain())
    M["xq"] = dv.n
    if "xq" in dbg:
        @blk.sync
        def _(s):
            s.wait_ge(dv.h, M["xq"])
            dS.dma(s.dma_start(out=dbg["xq"][:, :], in_=xq_s[:]))

    whole = tern_out.rearrange("(r q) w -> r q w", q=1536)

    def gu_slab_aps(mat, fb, s1):
        two0, woff = divmod(256 * fb, 4096)
        r0 = 512 * mat + 256 * s1 + two0
        src = whole[:, r0:r0 + 256:2, woff:woff + 256].rearrange(
            "r p f -> p r f")
        return src

    wd_view = []
    for rr in range(8):
        wd_view.append(
            tern_out[1536 * rr + 1024:1536 * rr + 1536, :].rearrange(
                "r (par o) -> (r par) o", par=2))

    for half in range(2):
        hof = 512 * half
        for fb in range(2):
            def mk(fb=fb, half=half):
                @blk.gpsimd
                def _(g):
                    g.wait_ge(cc.h, M["tern_ag"])
                    if half == 0 and fb == 0:
                        g.wait_ge(pe.h, M[("op", HC - 1)])
                    if half == 1:
                        g.wait_ge(pe.h, M[("gumm", 0, 61 + 2 * fb)])
                    for mat in range(2):
                        for s1 in range(2):
                            base = 8192 * (fb % 2) + 4096 * mat
                            dst = wgub[:, base:base + 4096].rearrange(
                                "p (r f2) -> p r f2", f2=512)[:, :, 256 * s1:
                                                             256 * (s1 + 1)]
                            dW.dma(g.dma_start(out=dst,
                                               in_=gu_slab_aps(mat, fb, s1)))
                M[("gsl", half, fb)] = dW.n
            mk()

        for f in range(64):
            fb, fc2 = divmod(f, 2)

            def mk(f=f, fb=fb, fc2=fc2, half=half, hof=hof):
                @blk.tensor
                def _(t):
                    t.wait_ge(dW.h, M[("gsl", half, fb)])
                    if half == 0 and f == 0:
                        t.wait_ge(dv.h, M["xq"])
                    if f >= 2:
                        t.wait_ge(dv.h, M[("gup", half, f - 2)])
                    elif half == 1:
                        t.wait_ge(dv.h, M[("gup", 0, 63 - (1 - f))])
                    for mat in range(2):
                        b = 2 * (f % 2) + mat
                        for c in range(HC):
                            base = 8192 * (fb % 2) + 4096 * mat
                            inst = t.matmul(
                                bank(b),
                                wgub[:, base + 256 * c + 128 * fc2:
                                     base + 256 * c + 128 * (fc2 + 1)],
                                xq_s[:, 1024 * c + hof:1024 * c + hof + 512],
                                start=(c == 0), stop=(c == HC - 1))
                    pe.inc(inst)
                M[("gumm", half, f)] = pe.n

                if fb + 2 < 32 and fc2 == 1:
                    @blk.gpsimd
                    def _(g):
                        g.wait_ge(pe.h, M[("gumm", half, f)])
                        for mat in range(2):
                            for s1 in range(2):
                                base = 8192 * (fb % 2) + 4096 * mat
                                dst = wgub[:, base:base + 4096].rearrange(
                                    "p (r f2) -> p r f2", f2=512)[
                                    :, :, 256 * s1:256 * (s1 + 1)]
                                dW.dma(g.dma_start(
                                    out=dst, in_=gu_slab_aps(mat, fb + 2, s1)))
                    M[("gsl", half, fb + 2)] = dW.n

                @blk.vector
                def _(v):
                    v.wait_ge(pe.h, M[("gumm", half, f)])
                    v.tensor_tensor(sgt[:], bank(2 * (f % 2)),
                                    bg_t[:, hof:hof + 512], op=ALU.mult)
                    v.tensor_tensor(upt[:], bank(2 * (f % 2) + 1),
                                    bu_t[:, hof:hof + 512], op=ALU.mult)
                    dv.inc(v.drain())
                M[("gup", half, f)] = dv.n

                @blk.scalar
                def _(s):
                    s.wait_ge(dv.h, M[("gup", half, f)])
                    s.activation(sgt[:], sgt[:], AF.Sigmoid)
                    ac.inc(s.drain())
                M[("sig", half, f)] = ac.n

                @blk.vector
                def _(v):
                    v.wait_ge(ac.h, M[("sig", half, f)])
                    if half == 0 and f == 0:
                        v.wait_ge(dS.h, M["x1_sp"])
                    sl = slice(512 * f, 512 * (f + 1))
                    v.tensor_tensor(h_s[:, sl], sgt[:], upt[:], op=ALU.mult)
                    if f == 0:
                        v.tensor_tensor(hacc[:], h_s[:, sl], h_s[:, sl],
                                        op=ALU.max)
                        inst = v.tensor_tensor(hacc2[:], h_s[:, sl], h_s[:, sl],
                                               op=ALU.min)
                    else:
                        v.tensor_tensor(hacc[:], hacc[:], h_s[:, sl],
                                        op=ALU.max)
                        inst = v.tensor_tensor(hacc2[:], hacc2[:], h_s[:, sl],
                                               op=ALU.min)
                    dv.inc(inst)
                M[("gu", half, f)] = dv.n
            mk()

        def mkhacc(half=half):
            @blk.vector
            def _(v):
                v.wait_ge(dv.h, M[("gu", half, 63)])
                v.tensor_scalar(hacc2[:], hacc2[:], -1.0, None, op0=ALU.mult)
                v.tensor_tensor(hacc[:], hacc[:], hacc2[:], op=ALU.max)
                dv.inc(v.drain())
            M[("qh_acc", half)] = dv.n
        mkhacc()

        quant_scales(("qh", half), 4, hacc, M[("qh_acc", half)], shb, ihb)

        def mkhq(half=half):
            @blk.vector
            def _(v):
                v.wait_ge(dv.h, M[(("qh", half), "done")])
                v.tensor_scalar(dsc[:], shb[:, 0:512], svec[:, 2:3], None,
                                op0=ALU.mult)
                dv.inc(v.drain())

            for k in range(8):
                def mkc(k=k):
                    @blk.vector
                    def _(v):
                        for f in range(8 * k, 8 * k + 8):
                            sl = slice(512 * f, 512 * (f + 1))
                            v.tensor_tensor(i8tmp[:, 0:512], h_s[:, sl],
                                            ihb[:, 0:512], op=ALU.mult)
                            v.tensor_copy(h_s[:, sl], i8tmp[:, 0:512])
                        dv.inc(v.drain())
                    M[("hqc", half, k)] = dv.n
                mkc()
            M[("hq", half)] = dv.n
        mkhq()
        if "h" in dbg and half == 0:
            @blk.sync
            def _(s):
                s.wait_ge(dv.h, M[("hq", 0)])
                dS.dma(s.dma_start(out=dbg["h"][:, :], in_=h_s[:]))

        for og in range(2):
            for f in range(2):
                def mk(f=f, og=og, half=half):
                    @blk.gpsimd
                    def _(g):
                        if og == 0 and f == 0:
                            if half == 0:
                                g.wait_ge(pe.h, M[("op", HC - 1)])
                            else:
                                g.wait_ge(pe.h, M[("dmm", 0, 1, 63)])
                        if og == 1 and f < 2:
                            g.wait_ge(pe.h, M[("dmm", half, 0, 63)])
                        rr, lf = divmod(f, 8)
                        dW.dma(g.dma_start(
                            out=wdb[:, 1024 * (f % 2):1024 * (f % 2) + 1024],
                            in_=wd_view[rr][128 * lf:128 * (lf + 1),
                                            1024 * og:1024 * (og + 1)]))
                    M[("wdl", half, og, f)] = dW.n
                mk()

            for f in range(64):
                def mk(f=f, og=og, half=half):
                    @blk.tensor
                    def _(t):
                        t.wait_ge(dW.h, M[("wdl", half, og, f)])
                        if og == 0:
                            t.wait_ge(dv.h, M[("hqc", half, f // 8)])
                        if og == 1 and f == 0:
                            t.wait_ge(dv.h, M[("dep", half, 0, 7)])
                        for oc8 in range(8):
                            inst = t.matmul(
                                bank(oc8),
                                wdb[:, 1024 * (f % 2) + 128 * oc8:
                                     1024 * (f % 2) + 128 * (oc8 + 1)],
                                h_s[:, 512 * f:512 * (f + 1)],
                                start=(f == 0), stop=(f == 63))
                        pe.inc(inst)
                    M[("dmm", half, og, f)] = pe.n

                    if f + 2 < 64:
                        @blk.gpsimd
                        def _(g):
                            g.wait_ge(pe.h, M[("dmm", half, og, f)])
                            rr, lf = divmod(f + 2, 8)
                            dW.dma(g.dma_start(
                                out=wdb[:, 1024 * (f % 2):1024 * (f % 2) + 1024],
                                in_=wd_view[rr][128 * lf:128 * (lf + 1),
                                                1024 * og:1024 * (og + 1)]))
                        M[("wdl", half, og, f + 2)] = dW.n
                mk()

            for oc8 in range(8):
                oc = 8 * og + oc8

                def mk(oc8=oc8, oc=oc, og=og, half=half, hof=hof):
                    m = 8 * og + oc8

                    @blk.sync
                    def _(s):
                        if m < 2:
                            s.wait_ge(dv.h, M[("hqc", half, 0)])
                        else:
                            s.wait_ge(dv.h, M[("dep", half, (m - 2) // 8,
                                               (m - 2) % 8)])
                        dS.dma(s.dma_start(
                            out=x1re[:, 512 * (m % 2):512 * (m % 2) + 512],
                            in_=x1_sc[128 * oc:128 * (oc + 1), hof:hof + 512]))
                    M[("x1r", half, og, oc8)] = dS.n

                    @blk.vector
                    def _(v):
                        v.wait_ge(pe.h, M[("dmm", half, og, 63)])
                        v.wait_ge(dS.h, M[("x1r", half, og, oc8)])
                        v.tensor_tensor(tmp1k[:, 0:512], bank(oc8), dsc[:, 0:512],
                                        op=ALU.mult)
                        v.tensor_tensor(r2_s[:, 512 * oc:512 * (oc + 1)],
                                        tmp1k[:, 0:512],
                                        x1re[:, 512 * (m % 2):512 * (m % 2) + 512],
                                        op=ALU.add)
                        dv.inc(v.drain())
                    M[("dep", half, og, oc8)] = dv.n
                mk()

        for oc in range(HC):
            def mk(oc=oc, half=half):
                ro2 = 512 * (oc % 2)

                @blk.vector
                def _(v):
                    if oc == 0:
                        v.wait_ge(dv.h, M[("dep", half, 1, 7)])
                    if oc >= 2:
                        v.wait_ge(pe.h, M[("st2", half, oc - 2)])
                    v.tensor_copy(rb2[:, ro2:ro2 + 512],
                                  r2_s[:, 512 * oc:512 * (oc + 1)])
                    dv.inc(v.drain())
                M[("rc2", half, oc)] = dv.n

                @blk.scalar
                def _(s):
                    s.wait_ge(dv.h, M[("rc2", half, oc)])
                    s.activation(r2b2[:, ro2:ro2 + 512],
                                 r2_s[:, 512 * oc:512 * (oc + 1)], AF.Square)
                    ac.inc(s.drain())
                M[("sq2", half, oc)] = ac.n

                @blk.tensor
                def _(t):
                    t.wait_ge(ac.h, M[("sq2", half, oc)])
                    t.matmul(bank(4), ones_b[:], rb2[:, ro2:ro2 + 512],
                             start=(oc == 0), stop=(oc == HC - 1))
                    inst = t.matmul(bank(6), ones_b[:], r2b2[:, ro2:ro2 + 512],
                                    start=(oc == 0), stop=(oc == HC - 1))
                    pe.inc(inst)
                M[("st2", half, oc)] = pe.n
            mk()

        def mkln2(half=half, hof=hof):
            @blk.vector
            def _(v):
                v.wait_ge(pe.h, M[("st2", half, HC - 1)])
                v.tensor_scalar(tmp1k[:, 0:512], bank(4), INV_D, None,
                                op0=ALU.mult)
                v.tensor_scalar(aln[:], bank(6), INV_D, None, op0=ALU.mult)
                v.tensor_tensor(cln[:], tmp1k[:, 0:512], tmp1k[:, 0:512],
                                op=ALU.mult)
                v.tensor_tensor(aln[:], aln[:], cln[:], op=ALU.subtract)
                v.tensor_scalar(aln[:], aln[:], LN_EPS, None, op0=ALU.add)
                v.reciprocal(cln[:], aln[:])
                dv.inc(v.drain())
            M[("ln2a", half)] = dv.n

            @blk.scalar
            def _(s):
                s.wait_ge(dv.h, M[("ln2a", half)])
                s.activation(aln[:], cln[:], AF.Sqrt)
                ac.inc(s.drain())
            M[("ln2b", half)] = ac.n

            @blk.vector
            def _(v):
                v.wait_ge(ac.h, M[("ln2b", half)])
                v.tensor_tensor(cln[:], tmp1k[:, 0:512], aln[:], op=ALU.mult)
                dv.inc(v.drain())
            M[("ln2c", half)] = dv.n

            for oc in range(HC):
                def mk2(oc=oc):
                    @blk.vector
                    def _(v):
                        if oc == 0:
                            v.wait_ge(dv.h, M[("ln2c", half)])
                        if oc >= 1:
                            v.wait_ge(dS.h, M[("ost", half, oc - 1)])
                        sl = slice(512 * oc, 512 * (oc + 1))
                        v.tensor_tensor(tmp1k[:, 512:1024], r2_s[:, sl], aln[:],
                                        op=ALU.mult)
                        v.tensor_tensor(obuf[:], tmp1k[:, 512:1024],
                                        cln[:], op=ALU.subtract)
                        dv.inc(v.drain())
                    M[("oap", half, oc)] = dv.n

                    @blk.sync
                    def _(s):
                        s.wait_ge(dv.h, M[("oap", half, oc)])
                        dS.dma(s.dma_start(
                            out=out[128 * oc:128 * (oc + 1), hof:hof + 512],
                            in_=obuf[:]))
                    M[("ost", half, oc)] = dS.n
                mk2()
        mkln2()

    @blk.sync
    def _(s):
        s.wait_ge(dS.h, dS.n)

    ctx.close()
    return nc


def _prep(inputs):
    x = np.asarray(inputs["L0_attn_hidden_states"], dtype=np.float32)
    w_qkv = np.asarray(inputs["L0_attn_w_qkv"], dtype=np.float32)
    b_qkv = np.asarray(inputs["L0_attn_b_qkv"], dtype=np.float32)
    w_out = np.asarray(inputs["L0_attn_w_out"], dtype=np.float32)
    b_out = np.asarray(inputs["L0_attn_b_out"], dtype=np.float32)
    w_g = np.asarray(inputs["L0_ffn_w_gate"], dtype=np.float32)
    w_u = np.asarray(inputs["L0_ffn_w_up"], dtype=np.float32)
    w_d = np.asarray(inputs["L0_ffn_w_down"], dtype=np.float32)

    xTf = np.ascontiguousarray(x[0].T)
    wqkvT = np.ascontiguousarray(w_qkv.T)
    woutT = np.ascontiguousarray(w_out.T)
    wgT = np.ascontiguousarray(w_g.T)
    wuT = np.ascontiguousarray(w_u.T)
    wdT = np.ascontiguousarray(w_d.T)
    bq = np.ascontiguousarray(b_qkv[0:D].reshape(HC, 128).T)
    bk = np.ascontiguousarray(b_qkv[D:2 * D].reshape(HC, 128).T)
    bo = np.ascontiguousarray(b_out.reshape(HC, 128).T)
    bvb = np.ascontiguousarray(np.broadcast_to(b_qkv[2 * D:3 * D], (128, D)))
    ident = np.eye(128, dtype=np.float32)

    maps = []
    for r in range(N_CORES):
        maps.append({
            "xT": np.ascontiguousarray(xTf[:, T * r:T * (r + 1)]),
            "wqkvT": wqkvT, "b_q": bq, "b_k": bk, "b_v_bc": bvb,
            "woutT": woutT, "b_o": bo,
            "wgT_sh": np.ascontiguousarray(wgT[256 * r:256 * (r + 1), :]),
            "wuT_sh": np.ascontiguousarray(wuT[256 * r:256 * (r + 1), :]),
            "wdT_sh": np.ascontiguousarray(wdT[1024 * r:1024 * (r + 1), :]),
            "identI": ident,
        })
    return maps


def kernel(**inputs):
    maps = _prep(inputs)
    nc = build(debug=tuple(DEBUG))
    res = run_bass_kernel_spmd(nc, maps, core_ids=list(range(N_CORES)))
    outs = [np.asarray(r["out"]) for r in res.results]
    full = np.concatenate([o.T for o in outs], axis=0)
    if DEBUG:
        kernel.last_results = res
    return np.ascontiguousarray(full.reshape(1, S_TOT, D), dtype=np.float32)
